# version 02
# baseline (speedup 1.0000x reference)
import sys
if "/opt/trn_rl_repo" not in sys.path:
    sys.path.insert(0, "/opt/trn_rl_repo")
import numpy as np
import ml_dtypes
import concourse.bass as bass
import concourse.mybir as mybir
from concourse.bass_utils import run_bass_kernel_spmd
from contextlib import ExitStack

F32 = mybir.dt.float32
BF16 = mybir.dt.bfloat16
AF = mybir.ActivationFunctionType
ALU = mybir.AluOpType
BF = ml_dtypes.bfloat16
ROW = 129


def recon_plan(n_ctrl, n):
    t = np.linspace(0.0, n_ctrl - 1.0, n, dtype=np.float64)
    i = np.clip(np.floor(t).astype(np.int64), 0, n_ctrl - 2)
    nrows = (n + ROW - 1) // ROW
    pad = nrows * ROW - n
    if pad:
        i = np.concatenate([i, np.full(pad, i[-1], np.int64)])
        t = np.concatenate([t, np.full(pad, t[-1], np.float64)])
    i_rows = i.reshape(nrows, ROW)
    t_rows = t.reshape(nrows, ROW)
    j_row = i_rows[:, 0]
    phi_row = t_rows[:, 0] - j_row
    delta = i_rows - j_row[:, None]
    assert delta.max() <= 1, "a row crosses more than one interval"
    cross_g = np.full(nrows, ROW, np.int64)
    has = delta[:, -1] == 1
    cross_g[has] = np.argmax(delta[has] > 0, axis=1)
    tiles = []
    r = 0
    while r < nrows:
        j0 = int(j_row[r])
        k = 1
        while r + k < nrows and k < 128 and j_row[r + k] == j0 + k:
            k += 1
        tiles.append([r, k, j0, False])
        r += k
    if len(tiles) % 2:
        tiles.append([nrows, 1, int(j_row[-1]), True])
    return dict(tiles=tiles, phi_row=phi_row, cross_g=cross_g, nrows=nrows,
                n=n, n_ctrl=n_ctrl)


def _c_poly():
    return np.array([
        [0.0, -0.5, 1.0, -0.5],
        [1.0, 0.0, -2.5, 1.5],
        [0.0, 0.5, 2.0, -1.5],
        [0.0, 0.0, -0.5, 0.5],
    ], dtype=np.float64)


def recon_tables(plan):
    n_ctrl, n = plan["n_ctrl"], plan["n"]
    sigma = (n_ctrl - 1.0) / (n - 1.0)
    from math import comb
    C = _c_poly()
    u = np.arange(ROW, dtype=np.float64) * sigma
    D = np.zeros((16, ROW))
    for w in range(4):
        d = np.zeros((4, 4))
        for k in range(4):
            for m in range(k + 1):
                d[m][k - m] += C[w, k] * comb(k, m)
        for m in range(4):
            D[4 * m + w] = sum(d[m][k] * u ** k for k in range(4))
    D = D.astype(np.float32)
    D1 = np.zeros((32, 2 * ROW), np.float32)
    D1[0:16, 0:ROW] = D
    D1[16:32, ROW:2 * ROW] = D
    D2 = np.zeros((64, ROW), np.float32)
    D2[0:16] = D; D2[16:32] = D; D2[32:48] = D; D2[48:64] = D

    tiles = plan["tiles"]
    ntiles = len(tiles)
    npairs = ntiles // 2
    nrows = plan["nrows"]
    phi_pad = np.concatenate([plan["phi_row"], np.zeros(300)])
    cg_pad = np.concatenate([plan["cross_g"], np.full(300, ROW)])
    PHIQ1 = np.zeros((npairs, 32, 128), np.float32)
    PHIQ2 = np.zeros((npairs, 64, 128), np.float32)
    gstar = np.zeros((128, ntiles), np.float32)
    for pi in range(npairs):
        for half in range(2):
            r0, nr, j0, dummy = tiles[2 * pi + half]
            ph = phi_pad[r0:r0 + 128]
            gstar[:, 2 * pi + half] = cg_pad[r0:r0 + 128]
            for m in range(4):
                pm = (ph ** m).astype(np.float32)
                qa = (-(ph ** m)).astype(np.float32)
                qb = ((ph - 1.0) ** m).astype(np.float32)
                for w in range(4):
                    PHIQ1[pi, 16 * half + 4 * m + w] = pm
                    PHIQ2[pi, 32 * half + 4 * m + w] = qa
                    PHIQ2[pi, 32 * half + 16 + 4 * m + w] = qb
    grow = np.broadcast_to(np.arange(ROW, dtype=np.float32), (128, ROW)).copy()
    return dict(D1=D1, D2=D2, PHIQ1=PHIQ1, PHIQ2=PHIQ2, gstar=gstar,
                grow=grow, sigma=sigma, npairs=npairs)


def make_cpx(cp):
    raise NotImplementedError


class Ctr:
    def __init__(self):
        self.v = {}

    def key(self, sem):
        return getattr(sem, "name", None) or id(sem)

    def inc(self, sem, n=1):
        self.v[self.key(sem)] = self.get(sem) + n

    def get(self, sem):
        return self.v.get(self.key(sem), 0)


def alloc_recon_bufs(nc, Ws, tbl, pb=None, arena=None):
    b = {}
    if arena is None:
        def mk(name, shape, dt):
            return mk(name, shape, dt)
    else:
        arF = arena.bitcast(F32)
        arB = arena
        state = {"byte": 0}

        def mk(name, shape, dt):
            P, C = shape
            b0 = (state["byte"] + 3) & ~3
            w = 4 if dt == F32 else 2
            state["byte"] = b0 + C * w
            if dt == F32:
                return arF[0:P, b0 // 4:b0 // 4 + C]
            return arB[0:P, b0 // 2:b0 // 2 + C]
    b["phiq1_dram"] = nc.dram_tensor("phiq1_t", list(tbl["PHIQ1"].shape), F32,
                                     kind="ExternalInput").ap()
    b["phiq2_dram"] = nc.dram_tensor("phiq2_t", list(tbl["PHIQ2"].shape), F32,
                                     kind="ExternalInput").ap()
    b["gstar_dram"] = nc.dram_tensor("gstar_t", list(tbl["gstar"].shape), F32,
                                     kind="ExternalInput").ap()
    b["grow_dram"] = nc.dram_tensor("grow_t", [128, ROW], F32,
                                    kind="ExternalInput").ap()
    b["d1_dram"] = nc.dram_tensor("d1_t", [32, 2 * ROW], F32,
                                  kind="ExternalInput").ap()
    b["d2_dram"] = nc.dram_tensor("d2_t", [64, ROW], F32,
                                  kind="ExternalInput").ap()
    b["s_phiq1"] = [mk(f"s_phiq1_{p}", [32, 128], F32)
                    for p in range(2)]
    b["s_phiq2"] = [mk(f"s_phiq2_{p}", [64, 128], F32)
                    for p in range(2)]
    b["s_cpw1"] = [{W: mk(f"s_cpw1_{p}_{W}", [32, 128], F32)
                    for W in Ws} for p in range(2)]
    b["s_cpw2"] = [{W: mk(f"s_cpw2_{p}_{W}", [64, 128], F32)
                    for W in Ws} for p in range(2)]
    b["s_a1"] = [{W: mk(f"s_a1_{p}_{W}", [32, 128], BF16)
                  for W in Ws} for p in range(2)]
    b["s_a2"] = [{W: mk(f"s_a2_{p}_{W}", [64, 128], BF16)
                  for W in Ws} for p in range(2)]
    b["s_d1"] = mk("s_d1", [32, 2 * ROW], F32)
    b["s_d2"] = mk("s_d2", [64, ROW], F32)
    b["s_d1b"] = mk("s_d1b", [32, 2 * ROW], BF16)
    b["s_d2b"] = mk("s_d2b", [64, ROW], BF16)
    b["s_grow"] = mk("s_grow", [128, ROW], F32)
    ntl = tbl["gstar"].shape[1]
    b["s_gstar"] = mk("s_gstar", [128, ntl], F32)
    b["s_mask"] = [mk(f"s_mask{p}", [128, 2 * ROW], F32)
                   for p in range(2)]
    b["s_t"] = mk("s_t", [128, 2 * ROW], F32)
    b["s_out"] = [{W: mk(f"s_out_{p}_{W}", [128, 2 * ROW], BF16)
                   for W in Ws} for p in range(2)]
    if pb is None:
        pb = [nc.alloc_psum_tensor(f"pbank{k}", [128, 512], F32)
              for k in range(8)]
    b["pbanks"] = pb
    b["p1"] = [{W: pb[p * 3 + wi][:, 0:258] for wi, W in enumerate(Ws)}
               for p in range(2)]
    b["p2a"] = [{W: pb[p * 3 + wi][:, 258:387] for wi, W in enumerate(Ws)}
                for p in range(2)]
    b["p2b"] = [{W: pb[6 + p][:, wi * 129:(wi + 1) * 129]
                 for wi, W in enumerate(Ws)} for p in range(2)]
    return b


def emit_recon_prologue(eng, bufs, sems, ctr):
    sync, vector = eng["sync"], eng["vector"]
    ldt, ab = sems["r_ldt"], sems["r_ab"]
    sync.dma_start(out=bufs["s_grow"][:, :], in_=bufs["grow_dram"]).then_inc(ldt, 16)
    sync.dma_start(out=bufs["s_gstar"][:, :], in_=bufs["gstar_dram"]).then_inc(ldt, 16)
    sync.dma_start(out=bufs["s_d1"][:, :], in_=bufs["d1_dram"]).then_inc(ldt, 16)
    sync.dma_start(out=bufs["s_d2"][:, :], in_=bufs["d2_dram"]).then_inc(ldt, 16)
    ctr.inc(ldt, 64)
    vector.wait_ge(ldt, ctr.get(ldt))
    vector.tensor_copy(bufs["s_d1b"][:, :], bufs["s_d1"][:, :])
    vector.tensor_copy(bufs["s_d2b"][:, :], bufs["s_d2"][:, :]).then_inc(ab, 1)
    ctr.inc(ab, 1)


def emit_recon(eng, bufs, plan, tbl, cp_aps, out_aps, sems, ctr,
               max_pairs=None):
    sync, vector, gpsimd, tensor = (eng["sync"], eng["vector"],
                                    eng["gpsimd"], eng["tensor"])
    tiles = plan["tiles"]
    npairs = tbl["npairs"] if max_pairs is None else min(tbl["npairs"], max_pairs)
    Ws = list(cp_aps.keys())
    NW = len(Ws)
    ab, mm, bl = sems["r_ab"], sems["r_mm"], sems["r_bl"]
    lds = [sems["r_ld0"], sems["r_ld1"]]
    sts = [sems["r_st0"], sems["r_st1"]]
    st_tot = []

    for pi in range(npairs):
        par = pi % 2
        r0a, nra, j0a, duma = tiles[2 * pi]
        r0b, nrb, j0b, dumb = tiles[2 * pi + 1]
        dj = j0b - j0a

        ld = lds[par]
        st = sts[par]
        if pi == 0:
            sync.wait_ge(sems["r_cpx"], ctr.get(sems["r_cpx"]))
        if pi >= 2:
            sync.wait_ge(ab, 1 + (pi - 1))
        sync.dma_start(out=bufs["s_phiq1"][par][:, :],
                       in_=bufs["phiq1_dram"][pi, :, :]).then_inc(ld, 16)
        sync.dma_start(out=bufs["s_phiq2"][par][:, :],
                       in_=bufs["phiq2_dram"][pi, :, :]).then_inc(ld, 16)
        ctr.inc(ld, 32)
        for W in Ws:
            cpx = cp_aps[W]
            for half, j0 in enumerate([j0a, j0b]):
                win = bass.AP(tensor=cpx.tensor, offset=cpx.offset + j0,
                              ap=[[0, 4], [1, 4], [1, 128]])
                sync.dma_start(
                    out=bufs["s_cpw1"][par][W][16 * half:16 * half + 16, :],
                    in_=win).then_inc(ld, 16)
                winB = bass.AP(tensor=cpx.tensor, offset=cpx.offset + j0 + 1,
                               ap=[[0, 4], [1, 4], [1, 128]])
                sync.dma_start(
                    out=bufs["s_cpw2"][par][W][32 * half:32 * half + 16, :],
                    in_=win).then_inc(ld, 16)
                sync.dma_start(
                    out=bufs["s_cpw2"][par][W][32 * half + 16:32 * half + 32, :],
                    in_=winB).then_inc(ld, 16)
                ctr.inc(ld, 48)

        gpsimd.wait_ge(ld, ctr.get(ld))
        if pi >= 2:
            gpsimd.wait_ge(mm, pi - 1)
        for wi, W in enumerate(Ws):
            gpsimd.tensor_tensor(bufs["s_a1"][par][W][:, :],
                                 bufs["s_cpw1"][par][W][:, :],
                                 bufs["s_phiq1"][par][:, :], ALU.mult)
            ins = gpsimd.tensor_tensor(bufs["s_a2"][par][W][:, :],
                                       bufs["s_cpw2"][par][W][:, :],
                                       bufs["s_phiq2"][par][:, :], ALU.mult)
            if wi == NW - 1:
                ins.then_inc(ab, 1)
        ctr.inc(ab, 1)

        tensor.wait_ge(ab, ctr.get(ab))
        if pi >= 2:
            tensor.wait_ge(bl, pi - 1)
        for wi, W in enumerate(Ws):
            tensor.matmul(bufs["p1"][par][W], bufs["s_a1"][par][W][0:32, :],
                          bufs["s_d1b"][:, :], start=True, stop=True,
                          skip_group_check=True)
            tensor.matmul(bufs["p2a"][par][W], bufs["s_a2"][par][W][0:32, :],
                          bufs["s_d2b"][0:32, :], start=True, stop=True,
                          skip_group_check=True)
            ins = tensor.matmul(bufs["p2b"][par][W], bufs["s_a2"][par][W][32:64, :],
                                bufs["s_d2b"][32:64, :], start=True, stop=True,
                                skip_group_check=True)
            if wi == NW - 1:
                ins.then_inc(mm, 1)
        ctr.inc(mm, 1)

        vector.wait_ge(mm, ctr.get(mm))
        if pi >= 2:
            vector.wait_ge(st, st_tot[pi - 2])
        mk = bufs["s_mask"][par]
        vector.tensor_scalar(mk[:, 0:ROW], bufs["s_grow"][:, :],
                             bufs["s_gstar"][:, 2 * pi:2 * pi + 1], None, ALU.is_ge)
        vector.tensor_scalar(mk[:, ROW:2 * ROW], bufs["s_grow"][:, :],
                             bufs["s_gstar"][:, 2 * pi + 1:2 * pi + 2], None,
                             ALU.is_ge)
        for wi, W in enumerate(Ws):
            t_s = bufs["s_t"]
            vector.tensor_tensor(t_s[:, 0:ROW], mk[:, 0:ROW],
                                 bufs["p2a"][par][W], ALU.mult)
            vector.tensor_tensor(t_s[:, ROW:2 * ROW], mk[:, ROW:2 * ROW],
                                 bufs["p2b"][par][W], ALU.mult)
            ins = vector.tensor_add(bufs["s_out"][par][W][:, :], t_s[:, :],
                                    bufs["p1"][par][W])
            if wi == NW - 1:
                ins.then_inc(bl, 1)
        ctr.inc(bl, 1)

        sync.wait_ge(bl, ctr.get(bl))
        for W in Ws:
            out = out_aps[W]
            for half, (r0, nr, dum) in enumerate(
                    [(r0a, nra, duma), (r0b, nrb, dumb)]):
                if dum:
                    continue
                src = bufs["s_out"][par][W][0:nr, half * ROW:(half + 1) * ROW]
                dst = bass.AP(tensor=out.tensor, offset=out.offset + r0 * ROW,
                              ap=[[ROW, nr], [1, ROW]])
                sync.dma_start(out=dst, in_=src).then_inc(st, 16)
                ctr.inc(st, 16)
        st_tot.append(ctr.get(st))

    sync.wait_ge(st, ctr.get(st))


CPX_PAD = 140


def emit_cpx_build(eng, cp_ap, cpx_ap, n_ctrl, sems, ctr):
    sync = eng["sync"]
    ld = sems["r_cpx"]
    sync.dma_start(out=cpx_ap[1:n_ctrl + 1], in_=cp_ap[:]).then_inc(ld, 16)
    sync.dma_start(out=cpx_ap[0:1], in_=cp_ap[0:1]).then_inc(ld, 16)
    npad = CPX_PAD - 1
    last = bass.AP(tensor=cp_ap.tensor, offset=cp_ap.offset + n_ctrl - 1,
                   ap=[[0, npad], [1, 1]])
    sync.dma_start(out=cpx_ap[n_ctrl + 1:n_ctrl + CPX_PAD]
                   .rearrange('(a b) -> a b', b=1), in_=last).then_inc(ld, 16)
    ctr.inc(ld, 48)


class _RecCall:
    def __init__(self, entry):
        self.entry = entry

    def then_inc(self, sem, n=1):
        self.entry[3].append((sem, n))
        return self


class _RecEngine:
    def __init__(self, script):
        self.script = script

    def __getattr__(self, meth):
        def call(*args, **kw):
            entry = [meth, args, kw, []]
            self.script.append(entry)
            return _RecCall(entry)
        return call


class Recorder:
    ENGINES = ("sync", "vector", "gpsimd", "tensor", "scalar")

    def __init__(self):
        self.scripts = {e: [] for e in self.ENGINES}

    def eng(self):
        return {e: _RecEngine(self.scripts[e]) for e in self.ENGINES}

    def replay(self, name, engine):
        for meth, args, kw, incs in self.scripts[name]:
            ins = getattr(engine, meth)(*args, **kw)
            for sem, n in incs:
                ins = ins.then_inc(sem, n)



D = 2048
R = 2048
DH = 512
INTER = 8192
OWN = 1024
PRE = 3072
NTOK = PRE + 2 + OWN
EPS = 1e-5

MAIN_SEMS = ["m_ldx0", "m_ldx1", "m_ldw0", "m_ldw1", "m_ldc", "m_st",
             "m_st0", "m_st1", "m_v", "m_a", "m_p", "m_t"]


def alloc_main(nc, v0_external_wbf=True):
    b = {}
    d = b
    d["x_own"] = nc.dram_tensor("x_own", [OWN + 2, D], F32, kind="ExternalInput").ap()
    d["x_pre"] = nc.dram_tensor("x_pre", [PRE, D], F32, kind="ExternalInput").ap()
    d["premask"] = nc.dram_tensor("premask", [128, 48], F32, kind="ExternalInput").ap()
    d["Wu"] = nc.dram_tensor("Wu", [R, D], F32, kind="ExternalInput").ap()
    d["Wv"] = nc.dram_tensor("Wv", [R, D], F32, kind="ExternalInput").ap()
    d["Wg1"] = nc.dram_tensor("Wg1", [DH, D], F32, kind="ExternalInput").ap()
    d["Wg2"] = nc.dram_tensor("Wg2", [R, DH], F32, kind="ExternalInput").ap()
    d["biases"] = nc.dram_tensor("biases", [128, 5 * 16], F32, kind="ExternalInput").ap()
    d["conv_w"] = nc.dram_tensor("conv_w", [128, 16 * 3], F32, kind="ExternalInput").ap()
    kind = "ExternalInput" if v0_external_wbf else "Internal"
    d["gate_bf"] = nc.dram_tensor("gate_bf", [INTER * D + 256], BF16, kind=kind).ap()
    d["up_bf"] = nc.dram_tensor("up_bf", [INTER * D + 256], BF16, kind=kind).ap()
    d["down_bf"] = nc.dram_tensor("down_bf", [INTER * D + 256], BF16, kind=kind).ap()
    d["out"] = nc.dram_tensor("out", [OWN, D], F32, kind="ExternalOutput").ap()

    d["Wu_bf"] = nc.dram_tensor("Wu_bf", [R, D], BF16).ap()
    d["Wv_bf"] = nc.dram_tensor("Wv_bf", [R, D], BF16).ap()
    d["Wg1_bf"] = nc.dram_tensor("Wg1_bf", [DH, D], BF16).ap()
    d["Wg2_bf"] = nc.dram_tensor("Wg2_bf", [R, DH], BF16).ap()
    d["WuT_pk"] = nc.dram_tensor("WuT_pk", [4, 16, 128, 512], BF16).ap()
    d["WvT_pk"] = nc.dram_tensor("WvT_pk", [4, 16, 128, 512], BF16).ap()
    d["WuT_dram"] = nc.dram_tensor("WuT_dram", [R, D], BF16).ap()
    d["xn_dram"] = nc.dram_tensor("xn_dram", [NTOK + 32, D], BF16).ap()
    d["tout_dram"] = nc.dram_tensor("tout_dram", [D * OWN], BF16).ap()
    d["xmid_dram"] = nc.dram_tensor("xmid_dram", [OWN, D], F32).ap()
    d["hn_dram"] = nc.dram_tensor("hn_dram", [OWN, D], BF16).ap()
    d["gwT_pk"] = nc.dram_tensor("gwT_pk", [16, 16, 128, 512], BF16).ap()
    d["upT_pk"] = nc.dram_tensor("upT_pk", [16, 16, 128, 512], BF16).ap()
    d["dwT"] = nc.dram_tensor("dwT", [INTER * D], BF16).ap()

    sb = nc.alloc_sbuf_tensor
    A = sb("arenaA", [128, 16384], F32)
    A16 = A.ap().bitcast(BF16)
    A32 = A.ap()
    Bar = sb("arenaB", [128, 8192], BF16).ap()
    Car = sb("arenaC", [128, 16896], BF16).ap()
    Ear = sb("arenaE", [128, 16384], BF16).ap()
    d["s_x"] = [A32[:, 0:2048], A32[:, 2048:4096]]
    d["s_sqv"] = A32[:, 4096:6144]
    d["s_sqa"] = A32[:, 6144:8192]
    d["s_xn"] = [A16[:, 16384:18432], A16[:, 18432:20480]]
    d["s_wupk"] = [A16[:, 0:8192], A16[:, 8192:16384]]
    d["s_wv"] = [A16[:, 16384:24576], A16[:, 24576:32768]]
    d["s_wun"] = [A16[:, 0:8192], A16[:, 8192:16384]]
    d["s_inter"] = A16[:, 0:32768]
    d["s_wtr"] = Bar[:, 0:8192]
    d["s_go"] = Bar[:, 0:8192]
    d["s_xnf"] = [Car[:, 0:8448], Car[:, 8448:16896]]
    d["_Car"] = Car
    d["s_gw"] = [Car[:, 0:8192], Car[:, 8192:16384]]
    d["s_wg1t"] = Ear[:, 0:8192]
    d["s_wg2t"] = Ear[:, 8192:16384]
    d["s_uw"] = [Ear[:, 0:8192], Ear[:, 8192:16384]]
    d["s_hnf"] = sb("s_hnf", [128, 16 * 512], BF16)
    d["s_lncol"] = sb("s_lncol", [128, 32], F32)
    d["s_premask"] = sb("s_premask", [128, 48], F32)
    d["s_colsum"] = sb("s_colsum", [128, 96], F32)
    d["s_off"] = sb("s_off", [128, 16], F32)
    d["s_carry"] = sb("s_carry", [128, 16], F32)
    d["s_bias"] = sb("s_bias", [128, 5 * 16], F32)
    d["s_convw"] = sb("s_convw", [128, 16 * 3], F32)
    d["s_usb"] = sb("s_usb", [128, 512], F32)
    d["s_gsb"] = sb("s_gsb", [128, 512], F32)
    d["s_vsb"] = sb("s_vsb", [128, 512], F32)
    d["s_ug"] = sb("s_ug", [128, 512], F32)
    d["s_vg"] = sb("s_vg", [128, 512], BF16)
    d["s_kc"] = sb("s_kc", [128, 512], F32)
    d["s_hf"] = sb("s_hf", [128, 4 * 512], BF16)
    d["s_hraw"] = sb("s_hraw", [128, 512], F32)
    d["s_ev"] = [sb(f"s_ev{i}", [128, 512], F32) for i in range(2)]
    d["s_tout"] = [sb(f"s_tout{i}", [128, 512], BF16) for i in range(2)]
    d["s_toutk"] = [sb(f"s_toutk{i}", [128, 512], BF16) for i in range(2)]
    d["s_xsl"] = [sb(f"s_xsl{i}", [128, 512], F32) for i in range(2)]
    d["s_dw"] = [sb(f"s_dw{i}", [128, D], BF16) for i in range(2)]
    d["s_silu"] = sb("s_silu", [128, 1024], F32)
    d["s_o"] = [sb(f"s_o{i}", [128, 512], F32) for i in range(2)]
    d["pb"] = [nc.alloc_psum_tensor(f"mpb{k}", [128, 512], F32) for k in range(8)]
    return b


def prologue(eng, b, sems, ctr, wbf_ready=0):
    sync, scalar, gpsimd = eng["sync"], eng["scalar"], eng["gpsimd"]
    ldc, ldw, st, g = sems["m_ldc"], sems["m_ldw0"], sems["m_st"], sems["m_p"]
    sync.dma_start(out=b["s_bias"][:, :], in_=b["biases"]).then_inc(ldc, 16)
    sync.dma_start(out=b["s_convw"][:, :], in_=b["conv_w"]).then_inc(ldc, 16)
    sync.dma_start(out=b["s_premask"][:, :], in_=b["premask"]).then_inc(ldc, 16)
    ctr.inc(ldc, 48)
    for w, wb in [("Wu", "Wu_bf"), ("Wv", "Wv_bf"), ("Wg1", "Wg1_bf"),
                  ("Wg2", "Wg2_bf")]:
        gpsimd.dma_start(out=b[wb], in_=b[w]).then_inc(g, 16)
        ctr.inc(g, 16)
    pdone = ctr.get(g)
    sync.wait_ge(g, pdone)

    def tr_store(in_ap, stage_cols, dst_ap):
        sync.wait_ge(st, ctr.get(st))
        sync.dma_start(out=b["s_wtr"][:, 0:stage_cols], in_=in_ap,
                       transpose=True).then_inc(ldw, 16)
        ctr.inc(ldw, 16)
        scalar.wait_ge(ldw, ctr.get(ldw))
        src = b["s_wtr"][:, 0:stage_cols]
        scalar.dma_start(out=dst_ap, in_=src).then_inc(st, 16)
        ctr.inc(st, 16)

    for k in range(16):
        sync.dma_start(out=b["s_wg1t"][:, k * DH:(k + 1) * DH],
                       in_=b["Wg1_bf"][0:DH, 128 * k:128 * k + 128],
                       transpose=True).then_inc(ldw, 16)
        ctr.inc(ldw, 16)
    for k in range(4):
        sync.dma_start(out=b["s_wg2t"][:, k * R:(k + 1) * R],
                       in_=b["Wg2_bf"][0:R, 128 * k:128 * k + 128],
                       transpose=True).then_inc(ldw, 16)
        ctr.inc(ldw, 16)
    for w, pk in [("Wu_bf", "WuT_pk"), ("Wv_bf", "WvT_pk")]:
        for k in range(16):
            dst = bass.AP(tensor=b[pk].tensor, offset=k * (128 * 512),
                          ap=[[512, 128], [16 * 128 * 512, 4], [1, 512]])
            sync.wait_ge(st, ctr.get(st))
            sync.dma_start(out=b["s_wtr"][:, 0:R],
                           in_=b[w][0:R, 128 * k:128 * k + 128],
                           transpose=True).then_inc(ldw, 16)
            ctr.inc(ldw, 16)
            scalar.wait_ge(ldw, ctr.get(ldw))
            src = b["s_wtr"][:, 0:R].rearrange("p (g c) -> p g c", g=4)
            scalar.dma_start(out=dst, in_=src).then_inc(st, 16)
            ctr.inc(st, 16)
    for k in range(16):
        tr_store(b["Wu_bf"][0:D, 128 * k:128 * k + 128], D,
                 b["WuT_dram"][128 * k:128 * k + 128, :])
    if True:
        gw = b["gate_bf"][0:INTER * D].rearrange("(i h) -> i h", h=D)
        uw = b["up_bf"][0:INTER * D].rearrange("(i h) -> i h", h=D)
        dw = b["down_bf"][0:INTER * D].rearrange("(h i) -> h i", i=INTER)
        dwt = b["dwT"].rearrange("(i h) -> i h", h=D)
        for wsrc, pk in [(gw, "gwT_pk"), (uw, "upT_pk")]:
            for k in range(16):
                dst = bass.AP(tensor=b[pk].tensor, offset=k * (128 * 512),
                              ap=[[512, 128], [16 * 128 * 512, 16], [1, 512]])
                sync.wait_ge(st, ctr.get(st))
                sync.dma_start(out=b["s_wtr"][:, :],
                               in_=wsrc[0:INTER, 128 * k:128 * k + 128],
                               transpose=True).then_inc(ldw, 16)
                ctr.inc(ldw, 16)
                scalar.wait_ge(ldw, ctr.get(ldw))
                src = b["s_wtr"][:, :].rearrange("p (g c) -> p g c", g=16)
                scalar.dma_start(out=dst, in_=src).then_inc(st, 16)
                ctr.inc(st, 16)
        for k in range(64):
            tr_store(dw[0:D, 128 * k:128 * k + 128], D,
                     dwt[128 * k:128 * k + 128, :])


def ln_stream(eng, b, sems, ctr, srcs, xn_dst, gate_sts=False):
    sync, scalar, vector = eng["sync"], eng["scalar"], eng["vector"]
    pool = eng["gpsimd"]
    v, a, p = sems["m_v"], sems["m_a"], sems["m_p"]
    if gate_sts:
        for snm in ["m_st", "m_st0", "m_st1"]:
            sync.wait_ge(sems[snm], ctr.get(sems[snm]))
    scalar.wait_ge(sems["m_ldc"], ctr.get(sems["m_ldc"]))
    tiles = []
    for ap, n, drow in srcs:
        r = 0
        while r < n:
            nr = min(128, n - r)
            tiles.append((ap, r, nr, drow + r))
            r += nr
    vmark, stmark = {}, {}
    for t, (ap, r0, nr, orow) in enumerate(tiles):
        par = t % 2
        ldx = sems[f"m_ldx{par}"]
        st = sems[f"m_st{par}"]
        sx, sxn, col = b["s_x"][par], b["s_xn"][par], b["s_lncol"]
        cb = 16 * par
        c = lambda k: col[0:nr, cb + k:cb + k + 1]
        if t >= 2:
            sync.wait_ge(v, vmark[t - 2])
        sync.dma_start(out=sx[0:nr, :], in_=ap[r0:r0 + nr, :]).then_inc(ldx, 16)
        ctr.inc(ldx, 16)
        ldn = ctr.get(ldx)
        scalar.wait_ge(ldx, ldn)
        scalar.activation(b["s_sqv"][0:nr, :], sx[0:nr, :], AF.Identity,
                          accum_out=c(0))
        scalar.activation(b["s_sqa"][0:nr, :], sx[0:nr, :], AF.Square,
                          accum_out=c(1)).then_inc(a, 1)
        ctr.inc(a, 1)
        vector.wait_ge(a, ctr.get(a))
        vector.tensor_scalar(c(2), c(0), -1.0 / D, None, ALU.mult)
        vector.tensor_scalar(c(5), c(1), 1.0 / D, EPS, ALU.mult,
                             ALU.add).then_inc(v, 1)
        ctr.inc(v, 1)
        scalar.wait_ge(v, ctr.get(v))
        scalar.activation(c(3), c(2), AF.Square).then_inc(a, 1)
        ctr.inc(a, 1)
        pool.wait_ge(a, ctr.get(a))
        pool.tensor_tensor(c(4), c(5), c(3), ALU.subtract).then_inc(p, 1)
        ctr.inc(p, 1)
        scalar.wait_ge(p, ctr.get(p))
        scalar.activation(c(6), c(4), AF.Ln).then_inc(a, 1)
        ctr.inc(a, 1)
        vector.wait_ge(a, ctr.get(a))
        vector.tensor_scalar(c(7), c(6), -0.5, None, ALU.mult).then_inc(v, 1)
        ctr.inc(v, 1)
        scalar.wait_ge(v, ctr.get(v))
        scalar.activation(c(8), c(7), AF.Exp).then_inc(a, 1)
        ctr.inc(a, 1)
        pool.wait_ge(a, ctr.get(a))
        pool.tensor_tensor(c(9), c(2), c(8), ALU.mult).then_inc(p, 1)
        ctr.inc(p, 1)
        vector.wait_ge(p, ctr.get(p))
        if t >= 2:
            vector.wait_ge(st, stmark[t - 2])
        vector.tensor_scalar(sxn[0:nr, :], sx[0:nr, :], c(8),
                             c(9), ALU.mult, ALU.add).then_inc(v, 1)
        ctr.inc(v, 1)
        vmark[t] = ctr.get(v)
        scalar.wait_ge(v, ctr.get(v))
        scalar.dma_start(out=xn_dst[orow:orow + nr, :],
                         in_=sxn[0:nr, :]).then_inc(st, 16)
        ctr.inc(st, 16)
        stmark[t] = ctr.get(st)


def attn_phase(eng, b, sems, ctr, cut=3, subcut=99, dbg=None):
    sync, scalar, vector, tensor = (eng["sync"], eng["scalar"], eng["vector"],
                                    eng["tensor"])
    v, a, t_s = sems["m_v"], sems["m_a"], sems["m_t"]
    pb = b["pb"]
    for snm in ["m_st", "m_st0", "m_st1"]:
        sync.wait_ge(sems[snm], ctr.get(sems[snm]))
    scalar.wait_ge(sems["m_ldc"], ctr.get(sems["m_ldc"]))
    vector.wait_ge(sems["m_ldc"], ctr.get(sems["m_ldc"]))
    toutd = b["tout_dram"].rearrange("(d t) -> d t", t=OWN)
    marks = {"uvdone": 0}
    if cut <= 0:
        return

    def do_ttile(ti, own):
        par = ti % 2
        ldx = sems[f"m_ldx{par}"]
        xnf = b["s_xnf"][par]
        if own:
            row0 = PRE + (ti - 6) * 512
            width = 528
        else:
            row0 = ti * 512
            width = 512
        if ti >= 2:
            sync.wait_ge(t_s, marks[("tdone", ti - 2)])
        for k in range(16):
            sync.dma_start(out=xnf[:, k * 528:k * 528 + width],
                           in_=b["xn_dram"][row0:row0 + width,
                                            128 * k:128 * k + 128],
                           transpose=True).then_inc(ldx, 16)
            ctr.inc(ldx, 16)
        xn_ld = ctr.get(ldx)
        off = 1 if own else 0
        if subcut <= 11:
            marks[("tdone", ti)] = ctr.get(t_s)
            return

        if own:
            for k in range(16):
                tp = k % 2
                stp = sems[f"m_st{tp}"]
                vector.wait_ge(ldx, xn_ld)
                if k >= 2:
                    vector.wait_ge(stp, marks[("tst", ti, k - 2)])
                cw = b["s_convw"]
                vector.tensor_scalar(b["s_ev"][0][:, :],
                                     xnf[:, k * 528:k * 528 + 512],
                                     cw[:, 3 * k:3 * k + 1],
                                     b["s_bias"][:, 48 + k:48 + k + 1],
                                     ALU.mult, ALU.add)
                vector.scalar_tensor_tensor(b["s_ev"][1][:, :],
                                            xnf[:, k * 528 + 1:k * 528 + 513],
                                            cw[:, 3 * k + 1:3 * k + 2],
                                            b["s_ev"][0][:, :], ALU.mult, ALU.add)
                vector.scalar_tensor_tensor(b["s_tout"][tp][:, :],
                                            xnf[:, k * 528 + 2:k * 528 + 514],
                                            cw[:, 3 * k + 2:3 * k + 3],
                                            b["s_ev"][1][:, :], ALU.mult,
                                            ALU.add).then_inc(v, 1)
                ctr.inc(v, 1)
                scalar.wait_ge(v, ctr.get(v))
                scalar.dma_start(
                    out=toutd[128 * k:128 * k + 128,
                              (ti - 6) * 512:(ti - 6) * 512 + 512],
                    in_=b["s_tout"][tp][:, :]).then_inc(stp, 16)
                ctr.inc(stp, 16)
                marks[("tst", ti, k)] = ctr.get(stp)

        tensor.wait_ge(ldx, xn_ld)
        tensor.wait_ge(v, ctr.get(v))
        if ti >= 7:
            tensor.wait_ge(v, marks["uvdone"])
        for m in range(4):
            for k in range(16):
                mi = tensor.matmul(pb[4 + m][:, :],
                                   b["s_wg1t"][:, k * DH + m * 128:k * DH + m * 128 + 128],
                                   xnf[:, k * 528 + off:k * 528 + off + 512],
                                   start=(k == 0), stop=(k == 15),
                                   skip_group_check=True)
        mi.then_inc(t_s, 1)
        ctr.inc(t_s, 1)
        if subcut <= 12:
            marks[("tdone", ti)] = ctr.get(t_s)
            return
        scalar.wait_ge(t_s, ctr.get(t_s))
        for m in range(4):
            scalar.activation(b["s_hraw"][:, :], pb[4 + m][:, :], AF.Identity,
                              bias=b["s_bias"][:, 64 + m:64 + m + 1])
            ins = scalar.activation(b["s_hf"][:, m * 512:(m + 1) * 512],
                                    b["s_hraw"][:, :], AF.Relu)
        ins.then_inc(a, 1)
        ctr.inc(a, 1)
        a_h = ctr.get(a)
        if subcut <= 13:
            tensor.wait_ge(a, a_h)
            marks[("tdone", ti)] = ctr.get(t_s)
            return

        for grp in range(4):
            wpar = grp % 2
            ldw = sems[f"m_ldw{wpar}"]
            wu = b["s_wupk"][wpar]
            wv = b["s_wv"][wpar]
            if grp >= 2:
                sync.wait_ge(t_s, marks[("gdone", ti, grp - 2)])
            elif ti == 7:
                sync.wait_ge(v, marks["uvdone"])
            elif ti >= 1:
                sync.wait_ge(t_s, marks[("gdone", ti - 1, grp + 2)])
            src = bass.AP(tensor=b["WuT_pk"].tensor, offset=grp * (16 * 128 * 512),
                          ap=[[512, 128], [128 * 512, 16], [1, 512]])
            sync.dma_start(out=wu[:, :], in_=src).then_inc(ldw, 16)
            ctr.inc(ldw, 16)
            if own:
                srcv = bass.AP(tensor=b["WvT_pk"].tensor,
                               offset=grp * (16 * 128 * 512),
                               ap=[[512, 128], [128 * 512, 16], [1, 512]])
                sync.dma_start(out=wv[:, :], in_=srcv).then_inc(ldw, 16)
                ctr.inc(ldw, 16)
            w_ld = ctr.get(ldw)
            tensor.wait_ge(ldw, w_ld)
            for j in range(4):
                r = grp * 4 + j
                for k in range(16):
                    tensor.matmul(pb[j][:, :],
                                  wu[:, k * 512 + j * 128:k * 512 + j * 128 + 128],
                                  xnf[:, k * 528 + off:k * 528 + off + 512],
                                  start=(k == 0), stop=(k == 15),
                                  skip_group_check=True)
                if own:
                    tensor.wait_ge(a, a_h)
                    if not (grp == 0 and j == 0):
                        tensor.wait_ge(v, ctr.get(v))
                    for k in range(16):
                        tensor.matmul(pb[4][:, :],
                                      wv[:, k * 512 + j * 128:k * 512 + j * 128 + 128],
                                      xnf[:, k * 528 + off:k * 528 + off + 512],
                                      start=(k == 0), stop=(k == 15),
                                      skip_group_check=True)
                tensor.wait_ge(a, a_h)
                for m in range(4):
                    mi = tensor.matmul(pb[5][:, :],
                                       b["s_wg2t"][:, m * R + r * 128:m * R + r * 128 + 128],
                                       b["s_hf"][:, m * 512:(m + 1) * 512],
                                       start=(m == 0), stop=(m == 3),
                                       skip_group_check=True)
                mi.then_inc(t_s, 1)
                ctr.inc(t_s, 1)
                scalar.wait_ge(t_s, ctr.get(t_s))
                scalar.activation(b["s_usb"][:, :], pb[j][:, :], AF.Identity,
                                  bias=b["s_bias"][:, r:r + 1])
                if own:
                    scalar.activation(b["s_vsb"][:, :], pb[4][:, :],
                                      AF.Identity,
                                      bias=b["s_bias"][:, 16 + r:16 + r + 1])
                scalar.activation(b["s_gsb"][:, :], pb[5][:, :], AF.Sigmoid,
                                  bias=b["s_bias"][:, 32 + r:32 + r + 1]
                                  ).then_inc(a, 1)
                ctr.inc(a, 1)
                if subcut <= 14:
                    tensor.wait_ge(a, ctr.get(a))
                    continue
                vector.wait_ge(a, ctr.get(a))
                if not own:
                    blk = ti // 2
                    half = ti % 2
                    cs = b["s_colsum"][:, half * 48 + blk * 16 + r:
                                       half * 48 + blk * 16 + r + 1]
                    vector.scalar_tensor_tensor(
                        b["s_ug"][:, :], b["s_usb"][:, :], 1.0, b["s_gsb"][:, :],
                        ALU.mult, ALU.mult, accum_out=cs).then_inc(v, 1)
                else:
                    vector.scalar_tensor_tensor(
                        b["s_ug"][:, :], b["s_usb"][:, :], 1.0, b["s_gsb"][:, :],
                        ALU.mult, ALU.mult)
                    vector.tensor_tensor(b["s_vg"][:, :], b["s_vsb"][:, :],
                                         b["s_gsb"][:, :], ALU.mult)
                    init = (b["s_off"][:, r:r + 1] if ti == 6
                            else b["s_carry"][:, r:r + 1])
                    vector.tensor_tensor_scan(b["s_kc"][:, :], b["s_ug"][:, :],
                                              b["s_ug"][:, :], init,
                                              ALU.add, ALU.bypass)
                    vector.tensor_tensor(b["s_go"][:, r * 512:(r + 1) * 512],
                                         b["s_kc"][:, :], b["s_vg"][:, :],
                                         ALU.mult)
                    vector.tensor_scalar(b["s_carry"][:, r:r + 1],
                                         b["s_kc"][:, 511:512], 1.0, None,
                                         ALU.mult).then_inc(v, 1)
                ctr.inc(v, 1)
                tensor.wait_ge(v, ctr.get(v))
            marks[("gdone", ti, grp)] = ctr.get(t_s)
        marks[("tdone", ti)] = ctr.get(t_s)

    for ti in range(6):
        do_ttile(ti, False)
    if subcut <= 14:
        return
    gpsimd = eng["gpsimd"]
    p_s = sems["m_p"]
    gpsimd.wait_ge(v, ctr.get(v))
    gpsimd.wait_ge(sems["m_ldc"], ctr.get(sems["m_ldc"]))
    gpsimd.tensor_tensor(b["s_colsum"][:, 0:48], b["s_colsum"][:, 0:48],
                         b["s_colsum"][:, 48:96], ALU.add)
    gpsimd.tensor_tensor(b["s_colsum"][:, 48:96], b["s_colsum"][:, 0:48],
                         b["s_premask"][:, 0:48], ALU.mult)
    gpsimd.tensor_tensor(b["s_off"][:, :], b["s_colsum"][:, 48:64],
                         b["s_colsum"][:, 64:80], ALU.add)
    gpsimd.tensor_tensor(b["s_off"][:, :], b["s_off"][:, :],
                         b["s_colsum"][:, 80:96], ALU.add).then_inc(p_s, 1)
    ctr.inc(p_s, 1)
    off_p = ctr.get(p_s)
    if cut <= 1:
        return

    vector.wait_ge(sems["m_p"], ctr.get(sems["m_p"]))
    for ti in range(6, 8):
        do_ttile(ti, True)
        tt = ti - 6
        if dbg is not None and ti == 6:
            sync.wait_ge(v, ctr.get(v))
            sync.dma_start(out=dbg["go6"], in_=b["s_go"][:, :]).then_inc(
                sems["m_ldc"], 16)
            sync.dma_start(out=dbg["carry6"], in_=b["s_carry"][:, :]).then_inc(
                sems["m_ldc"], 16)
            ctr.inc(sems["m_ldc"], 32)
        if cut <= 2:
            continue
        sync.wait_ge(sems["m_st0"], marks[("tst", ti, 14)])
        sync.wait_ge(sems["m_st1"], marks[("tst", ti, 15)])
        for dt in range(4):
            wpar = dt % 2
            ldw = sems[f"m_ldw{wpar}"]
            if dt >= 2 or tt > 0:
                sync.wait_ge(t_s, ctr.get(t_s))
            else:
                sync.wait_ge(t_s, marks[("tdone", ti)])
            src = bass.AP(tensor=b["WuT_dram"].tensor, offset=dt * 512,
                          ap=[[2048, 128], [128 * 2048, 16], [1, 512]])
            sync.dma_start(out=b["s_wun"][wpar][:, :], in_=src).then_inc(ldw, 16)
            ctr.inc(ldw, 16)
            wld = ctr.get(ldw)
            for tc in range(4):
                ip = tc % 2
                ldx = sems[f"m_ldx{ip}"]
                if dt + tc > 0:
                    sync.wait_ge(v, ctr.get(v))
                sync.dma_start(out=b["s_toutk"][ip][:, :],
                               in_=toutd[dt * 512:dt * 512 + 512,
                                         tt * 512 + tc * 128:
                                         tt * 512 + tc * 128 + 128],
                               transpose=True).then_inc(ldx, 16)
                sync.dma_start(out=b["s_xsl"][ip][:, :],
                               in_=b["x_own"][1 + tt * 512 + tc * 128:
                                              1 + tt * 512 + tc * 128 + 128,
                                              dt * 512:dt * 512 + 512]).then_inc(ldx, 16)
                ctr.inc(ldx, 32)
                xld = ctr.get(ldx)
                tensor.wait_ge(ldw, wld)
                if dt + tc > 0:
                    tensor.wait_ge(v, ctr.get(v))
                for rch in range(16):
                    mi = tensor.matmul(pb[7][:, :],
                                       b["s_go"][:, rch * 512 + tc * 128:
                                                 rch * 512 + tc * 128 + 128],
                                       b["s_wun"][wpar][:, rch * 512:(rch + 1) * 512],
                                       start=(rch == 0), stop=(rch == 15),
                                       skip_group_check=True)
                mi.then_inc(t_s, 1)
                ctr.inc(t_s, 1)
                vector.wait_ge(t_s, ctr.get(t_s))
                vector.wait_ge(ldx, xld)
                if dt + tc > 0:
                    vector.wait_ge(sems["m_st"], ctr.get(sems["m_st"]))
                vector.tensor_tensor(b["s_ev"][0][:, :], b["s_toutk"][ip][:, :],
                                     pb[7][:, :], ALU.add)
                vector.tensor_tensor(b["s_o"][ip][:, :], b["s_ev"][0][:, :],
                                     b["s_xsl"][ip][:, :], ALU.add).then_inc(v, 1)
                ctr.inc(v, 1)
                marks["uvdone"] = ctr.get(v)
                scalar.wait_ge(v, ctr.get(v))
                scalar.dma_start(
                    out=b["xmid_dram"][tt * 512 + tc * 128:tt * 512 + tc * 128 + 128,
                                       dt * 512:dt * 512 + 512],
                    in_=b["s_o"][ip][:, :]).then_inc(sems["m_st"], 16)
                ctr.inc(sems["m_st"], 16)


def ffn_phase(eng, b, sems, ctr):
    sync, scalar, vector, tensor = (eng["sync"], eng["scalar"], eng["vector"],
                                    eng["tensor"])
    v, a, t_s = sems["m_v"], sems["m_a"], sems["m_t"]
    pb = b["pb"]
    dwt = b["dwT"].rearrange("(i h) -> i h", h=D)
    for snm in ["m_st", "m_st0", "m_st1"]:
        sync.wait_ge(sems[snm], ctr.get(sems[snm]))
    for sup in range(2):
        if sup > 0:
            sync.wait_ge(t_s, ctr.get(t_s))
        ldx = sems["m_ldx0"]
        for k in range(16):
            sync.dma_start(out=b["s_hnf"][:, k * 512:(k + 1) * 512],
                           in_=b["hn_dram"][sup * 512:sup * 512 + 512,
                                            128 * k:128 * k + 128],
                           transpose=True).then_inc(ldx, 16)
            ctr.inc(ldx, 16)
        hn_ld = ctr.get(ldx)
        for ig in range(16):
            wpar = ig % 2
            ldw = sems[f"m_ldw{wpar}"]
            gw, uw = b["s_gw"][wpar], b["s_uw"][wpar]
            if ig >= 2:
                sync.wait_ge(t_s, ctr.get(t_s))
            for pk, dst in [("gwT_pk", gw), ("upT_pk", uw)]:
                srcap = bass.AP(tensor=b[pk].tensor, offset=ig * (16 * 128 * 512),
                                ap=[[512, 128], [128 * 512, 16], [1, 512]])
                sync.dma_start(out=dst[:, :], in_=srcap).then_inc(ldw, 16)
                ctr.inc(ldw, 16)
            wld = ctr.get(ldw)
            tensor.wait_ge(ldw, wld)
            tensor.wait_ge(ldx, hn_ld)
            if ig >= 1:
                tensor.wait_ge(v, ctr.get(v))
            for ic in range(4):
                for k in range(16):
                    tensor.matmul(pb[ic][:, :],
                                  gw[:, k * 512 + ic * 128:k * 512 + ic * 128 + 128],
                                  b["s_hnf"][:, k * 512:(k + 1) * 512],
                                  start=(k == 0), stop=(k == 15),
                                  skip_group_check=True)
                for k in range(16):
                    mi = tensor.matmul(pb[4 + ic][:, :],
                                       uw[:, k * 512 + ic * 128:k * 512 + ic * 128 + 128],
                                       b["s_hnf"][:, k * 512:(k + 1) * 512],
                                       start=(k == 0), stop=(k == 15),
                                       skip_group_check=True)
            mi.then_inc(t_s, 1)
            ctr.inc(t_s, 1)
            scalar.wait_ge(t_s, ctr.get(t_s))
            vector.wait_ge(t_s, ctr.get(t_s))
            vmk = {}
            for ic in range(4):
                i = ig * 4 + ic
                sp = ic % 2
                if ic >= 2:
                    scalar.wait_ge(v, vmk[ic - 2])
                scalar.activation(b["s_silu"][:, sp * 512:sp * 512 + 512],
                                  pb[ic][:, :], AF.Silu).then_inc(a, 1)
                ctr.inc(a, 1)
                vector.wait_ge(a, ctr.get(a))
                vector.tensor_tensor(b["s_inter"][:, i * 512:(i + 1) * 512],
                                     b["s_silu"][:, sp * 512:sp * 512 + 512],
                                     pb[4 + ic][:, :],
                                     ALU.mult).then_inc(v, 1)
                ctr.inc(v, 1)
                vmk[ic] = ctr.get(v)
        inter_v = ctr.get(v)
        for hh in range(2):
            tensor.wait_ge(v, inter_v)
            if hh > 0 or sup > 0:
                tensor.wait_ge(v, ctr.get(v))
            for i in range(64):
                dpar = i % 2
                ldw = sems[f"m_ldw{dpar}"]
                if i >= 2:
                    sync.wait_ge(t_s, marks_dw[i - 2])
                sync.dma_start(out=b["s_dw"][dpar][:, :],
                               in_=dwt[128 * i:128 * i + 128, :]).then_inc(ldw, 16)
                ctr.inc(ldw, 16)
                dld = ctr.get(ldw)
                tensor.wait_ge(ldw, dld)
                for ht in range(2):
                    h4 = hh * 2 + ht
                    for tc in range(4):
                        mi = tensor.matmul(pb[tc * 2 + ht][:, :],
                                           b["s_inter"][:, i * 512 + tc * 128:
                                                        i * 512 + tc * 128 + 128],
                                           b["s_dw"][dpar][:, h4 * 512:h4 * 512 + 512],
                                           start=(i == 0), stop=(i == 63),
                                           skip_group_check=True)
                mi.then_inc(t_s, 1)
                ctr.inc(t_s, 1)
                marks_dw[i] = ctr.get(t_s)
            for ht in range(2):
                h4 = hh * 2 + ht
                for tc in range(4):
                    ip = tc % 2
                    if True:
                        sync.wait_ge(v, ctr.get(v))
                    sync.dma_start(out=b["s_xsl"][ip][:, :],
                                   in_=b["xmid_dram"][sup * 512 + tc * 128:
                                                      sup * 512 + tc * 128 + 128,
                                                      h4 * 512:h4 * 512 + 512])                         .then_inc(ldx, 16)
                    ctr.inc(ldx, 16)
                    vector.wait_ge(ldx, ctr.get(ldx))
                    vector.wait_ge(t_s, ctr.get(t_s))
                    if True:
                        vector.wait_ge(sems["m_st"], ctr.get(sems["m_st"]))
                    vector.tensor_tensor(b["s_o"][ip][:, :],
                                         pb[tc * 2 + ht][:, :],
                                         b["s_xsl"][ip][:, :],
                                         ALU.add).then_inc(v, 1)
                    ctr.inc(v, 1)
                    scalar.wait_ge(v, ctr.get(v))
                    scalar.dma_start(
                        out=b["out"][sup * 512 + tc * 128:sup * 512 + tc * 128 + 128,
                                     h4 * 512:h4 * 512 + 512],
                        in_=b["s_o"][ip][:, :]).then_inc(sems["m_st"], 16)
                    ctr.inc(sems["m_st"], 16)


marks_dw = {}


def finale(eng, sems, ctr):
    for snm in ["m_st", "m_st0", "m_st1"]:
        eng["scalar"].wait_ge(sems[snm], ctr.get(sems[snm]))
        eng["sync"].wait_ge(sems[snm], ctr.get(sems[snm]))


N_CTRL = 130156
NSAMP = INTER * D
RECON_SEMS = ["r_cpx", "r_ldt", "r_ld0", "r_ld1", "r_ab", "r_mm", "r_bl",
              "r_st0", "r_st1"]
Ws = ["g", "u", "d"]


def _replay_all(nc, rec):
    with nc.Block() as block:
        @block.sync
        def _(sync):
            rec.replay("sync", sync)

        @block.vector
        def _(vector):
            rec.replay("vector", vector)

        @block.gpsimd
        def _(gpsimd):
            rec.replay("gpsimd", gpsimd)

        @block.tensor
        def _(tensor):
            rec.replay("tensor", tensor)

        @block.scalar
        def _(scalar):
            rec.replay("scalar", scalar)


def _build_recon(plan, tbl):
    nc = bass.Bass("TRN2", target_bir_lowering=False)
    cp_in = {W: nc.dram_tensor(f"cp_{W}", [N_CTRL], F32,
                               kind="ExternalInput").ap() for W in Ws}
    cpx = {W: nc.dram_tensor(f"cpx_{W}", [N_CTRL + CPX_PAD], F32).ap()
           for W in Ws}
    nrows = plan["nrows"]
    wout = {W: nc.dram_tensor(f"w_{W}", [nrows * ROW], BF16,
                              kind="ExternalOutput").ap() for W in Ws}
    ctx = ExitStack()
    sems = {nm: ctx.enter_context(nc.semaphore(nm)) for nm in RECON_SEMS}
    ctr = Ctr()
    bufs = alloc_recon_bufs(nc, Ws, tbl)
    rec = Recorder()
    eng = rec.eng()
    for W in Ws:
        emit_cpx_build(eng, cp_in[W], cpx[W], N_CTRL, sems, ctr)
    emit_recon_prologue(eng, bufs, sems, ctr)
    emit_recon(eng, bufs, plan, tbl, cpx, wout, sems, ctr)
    _replay_all(nc, rec)
    ctx.close()
    return nc


def _build_main():
    nc = bass.Bass("TRN2", target_bir_lowering=False)
    ctx = ExitStack()
    sems = {nm: ctx.enter_context(nc.semaphore(nm)) for nm in MAIN_SEMS}
    ctr = Ctr()
    b = alloc_main(nc, v0_external_wbf=True)
    rec = Recorder()
    eng = rec.eng()
    prologue(eng, b, sems, ctr)
    ln_stream(eng, b, sems, ctr,
              [(b["x_pre"], PRE, 0), (b["x_own"], OWN + 2, PRE)],
              b["xn_dram"])
    attn_phase(eng, b, sems, ctr)
    ln_stream(eng, b, sems, ctr, [(b["xmid_dram"], OWN, 0)], b["hn_dram"],
              gate_sts=True)
    ffn_phase(eng, b, sems, ctr)
    finale(eng, sems, ctr)
    _replay_all(nc, rec)
    ctx.close()
    return nc


def _pack_core_inputs(c, x, Wu, bu, Wv, bv, Wg1, bg1, Wg2, bg2, cw, cb,
                      gate_bf, up_bf, down_bf):
    bat, blk = c // 4, c % 4
    t0 = blk * 1024
    L = 4096
    x_own = np.zeros((OWN + 2, D), np.float32)
    j0, j1 = max(0, t0 - 1), min(L, t0 + OWN + 1)
    x_own[j0 - (t0 - 1):j1 - (t0 - 1)] = x[bat, j0:j1]
    x_pre = np.zeros((PRE, D), np.float32)
    x_pre[0:t0] = x[bat, 0:t0]
    premask = np.zeros((128, 48), np.float32)
    for w in range(blk):
        premask[:, w * 16:(w + 1) * 16] = 1.0
    biases = np.zeros((128, 80), np.float32)
    biases[:, 0:16] = bu.reshape(16, 128).T
    biases[:, 16:32] = bv.reshape(16, 128).T
    biases[:, 32:48] = bg2.reshape(16, 128).T
    biases[:, 48:64] = cb.reshape(16, 128).T
    biases[:, 64:68] = bg1.reshape(4, 128).T
    convw = np.zeros((128, 48), np.float32)
    for k in range(16):
        convw[:, 3 * k:3 * k + 3] = cw[128 * k:128 * k + 128, :]
    return {
        "x_own": x_own, "x_pre": x_pre, "premask": premask,
        "Wu": Wu, "Wv": Wv, "Wg1": Wg1, "Wg2": Wg2,
        "biases": biases, "conv_w": convw,
        "gate_bf": gate_bf, "up_bf": up_bf, "down_bf": down_bf,
    }


def kernel(**inputs):
    x = np.asarray(inputs["L0_attn_x"], np.float32)
    Wu = np.asarray(inputs["L0_attn_Wu"], np.float32)
    bu = np.asarray(inputs["L0_attn_bu"], np.float32)
    Wv = np.asarray(inputs["L0_attn_Wv"], np.float32)
    bv = np.asarray(inputs["L0_attn_bv"], np.float32)
    Wg1 = np.asarray(inputs["L0_attn_Wg1"], np.float32)
    bg1 = np.asarray(inputs["L0_attn_bg1"], np.float32)
    Wg2 = np.asarray(inputs["L0_attn_Wg2"], np.float32)
    bg2 = np.asarray(inputs["L0_attn_bg2"], np.float32)
    cw = np.asarray(inputs["L0_attn_conv_w"], np.float32)[:, 0, :]
    cb = np.asarray(inputs["L0_attn_conv_b"], np.float32)
    cps = {"g": np.asarray(inputs["L0_ffn_gate_cp"], np.float32),
           "u": np.asarray(inputs["L0_ffn_up_cp"], np.float32),
           "d": np.asarray(inputs["L0_ffn_down_cp"], np.float32)}

    plan = recon_plan(N_CTRL, NSAMP)
    tbl = recon_tables(plan)
    nc_r = _build_recon(plan, tbl)
    im_r = {f"cp_{W}": cps[W] for W in Ws}
    im_r.update(phiq1_t=tbl["PHIQ1"], phiq2_t=tbl["PHIQ2"],
                gstar_t=tbl["gstar"], grow_t=tbl["grow"],
                d1_t=tbl["D1"], d2_t=tbl["D2"])
    res_r = run_bass_kernel_spmd(nc_r, [im_r] * 8, core_ids=list(range(8)))
    pad = np.zeros(256, BF)
    wbf = {W: np.concatenate([np.asarray(res_r.results[0][f"w_{W}"])[:NSAMP],
                              pad]) for W in Ws}

    nc_m = _build_main()
    ims = [_pack_core_inputs(c, x, Wu, bu, Wv, bv, Wg1, bg1, Wg2, bg2,
                             cw, cb, wbf["g"], wbf["u"], wbf["d"])
           for c in range(8)]
    res = run_bass_kernel_spmd(nc_m, ims, core_ids=list(range(8)))
    y = np.zeros((2, 4096, 2048), np.float32)
    for c in range(8):
        bat, blk = c // 4, c % 4
        y[bat, blk * 1024:(blk + 1) * 1024] = np.asarray(res.results[c]["out"])
    return y


def _build_fused(plan, tbl):
    nc = bass.Bass("TRN2", target_bir_lowering=False)
    ctx = ExitStack()
    sems = {nm: ctx.enter_context(nc.semaphore(nm))
            for nm in RECON_SEMS + MAIN_SEMS}
    ctr = Ctr()
    b = alloc_main(nc, v0_external_wbf=False)
    cp_in = {W: nc.dram_tensor(f"cp_{W}", [N_CTRL], F32,
                               kind="ExternalInput").ap() for W in Ws}
    cpx = {W: nc.dram_tensor(f"cpx_{W}", [N_CTRL + CPX_PAD], F32).ap()
           for W in Ws}
    nrows = plan["nrows"]
    wdst = {"g": b["gate_bf"], "u": b["up_bf"], "d": b["down_bf"]}
    wout = {W: wdst[W][0:nrows * ROW] for W in Ws}
    bufs = alloc_recon_bufs(nc, Ws, tbl, pb=b["pb"], arena=b["_Car"])
    rec = Recorder()
    eng = rec.eng()
    for W in Ws:
        emit_cpx_build(eng, cp_in[W], cpx[W], N_CTRL, sems, ctr)
    emit_recon_prologue(eng, bufs, sems, ctr)
    emit_recon(eng, bufs, plan, tbl, cpx, wout, sems, ctr)
    for e in ("sync", "scalar", "vector", "tensor", "gpsimd"):
        eng[e].wait_ge(sems["r_st0"], ctr.get(sems["r_st0"]))
        eng[e].wait_ge(sems["r_st1"], ctr.get(sems["r_st1"]))
    prologue(eng, b, sems, ctr)
    ln_stream(eng, b, sems, ctr,
              [(b["x_pre"], PRE, 0), (b["x_own"], OWN + 2, PRE)],
              b["xn_dram"])
    attn_phase(eng, b, sems, ctr)
    ln_stream(eng, b, sems, ctr, [(b["xmid_dram"], OWN, 0)], b["hn_dram"],
              gate_sts=True)
    ffn_phase(eng, b, sems, ctr)
    finale(eng, sems, ctr)
    _replay_all(nc, rec)
    ctx.close()
    return nc


def kernel(**inputs):
    x = np.asarray(inputs["L0_attn_x"], np.float32)
    Wu = np.asarray(inputs["L0_attn_Wu"], np.float32)
    bu = np.asarray(inputs["L0_attn_bu"], np.float32)
    Wv = np.asarray(inputs["L0_attn_Wv"], np.float32)
    bv = np.asarray(inputs["L0_attn_bv"], np.float32)
    Wg1 = np.asarray(inputs["L0_attn_Wg1"], np.float32)
    bg1 = np.asarray(inputs["L0_attn_bg1"], np.float32)
    Wg2 = np.asarray(inputs["L0_attn_Wg2"], np.float32)
    bg2 = np.asarray(inputs["L0_attn_bg2"], np.float32)
    cw = np.asarray(inputs["L0_attn_conv_w"], np.float32)[:, 0, :]
    cb = np.asarray(inputs["L0_attn_conv_b"], np.float32)
    cps = {"g": np.asarray(inputs["L0_ffn_gate_cp"], np.float32),
           "u": np.asarray(inputs["L0_ffn_up_cp"], np.float32),
           "d": np.asarray(inputs["L0_ffn_down_cp"], np.float32)}

    plan = recon_plan(N_CTRL, NSAMP)
    tbl = recon_tables(plan)
    nc = _build_fused(plan, tbl)
    zero_bf = np.zeros(1, BF)
    ims = []
    for c in range(8):
        im = _pack_core_inputs(c, x, Wu, bu, Wv, bv, Wg1, bg1, Wg2, bg2,
                               cw, cb, zero_bf, zero_bf, zero_bf)
        for k in ("gate_bf", "up_bf", "down_bf"):
            del im[k]
        im.update({f"cp_{W}": cps[W] for W in Ws})
        im.update(phiq1_t=tbl["PHIQ1"], phiq2_t=tbl["PHIQ2"],
                  gstar_t=tbl["gstar"], grow_t=tbl["grow"],
                  d1_t=tbl["D1"], d2_t=tbl["D2"])
        ims.append(im)
    res = run_bass_kernel_spmd(nc, ims, core_ids=list(range(8)))
    y = np.zeros((2, 4096, 2048), np.float32)
    for c in range(8):
        bat, blk = c // 4, c % 4
        y[bat, blk * 1024:(blk + 1) * 1024] = np.asarray(res.results[c]["out"])
    return y


# version 03 (final)
# speedup vs baseline: 1.2739x; 1.2739x over previous
import sys
if "/opt/trn_rl_repo" not in sys.path:
    sys.path.insert(0, "/opt/trn_rl_repo")
import numpy as np
import ml_dtypes
import concourse.bass as bass
import concourse.mybir as mybir
from concourse.bass_utils import run_bass_kernel_spmd
from contextlib import ExitStack

F32 = mybir.dt.float32
BF16 = mybir.dt.bfloat16
AF = mybir.ActivationFunctionType
ALU = mybir.AluOpType
BF = ml_dtypes.bfloat16
ROW = 129


def recon_plan(n_ctrl, n):
    t = np.linspace(0.0, n_ctrl - 1.0, n, dtype=np.float64)
    i = np.clip(np.floor(t).astype(np.int64), 0, n_ctrl - 2)
    nrows = (n + ROW - 1) // ROW
    pad = nrows * ROW - n
    if pad:
        i = np.concatenate([i, np.full(pad, i[-1], np.int64)])
        t = np.concatenate([t, np.full(pad, t[-1], np.float64)])
    i_rows = i.reshape(nrows, ROW)
    t_rows = t.reshape(nrows, ROW)
    j_row = i_rows[:, 0]
    phi_row = t_rows[:, 0] - j_row
    delta = i_rows - j_row[:, None]
    assert delta.max() <= 1, "a row crosses more than one interval"
    cross_g = np.full(nrows, ROW, np.int64)
    has = delta[:, -1] == 1
    cross_g[has] = np.argmax(delta[has] > 0, axis=1)
    tiles = []
    r = 0
    while r < nrows:
        j0 = int(j_row[r])
        k = 1
        while r + k < nrows and k < 128 and j_row[r + k] == j0 + k:
            k += 1
        tiles.append([r, k, j0, False])
        r += k
    if len(tiles) % 2:
        tiles.append([nrows, 1, int(j_row[-1]), True])
    return dict(tiles=tiles, phi_row=phi_row, cross_g=cross_g, nrows=nrows,
                n=n, n_ctrl=n_ctrl)


def _c_poly():
    return np.array([
        [0.0, -0.5, 1.0, -0.5],
        [1.0, 0.0, -2.5, 1.5],
        [0.0, 0.5, 2.0, -1.5],
        [0.0, 0.0, -0.5, 0.5],
    ], dtype=np.float64)


def recon_tables(plan):
    n_ctrl, n = plan["n_ctrl"], plan["n"]
    sigma = (n_ctrl - 1.0) / (n - 1.0)
    from math import comb
    C = _c_poly()
    u = np.arange(ROW, dtype=np.float64) * sigma
    D = np.zeros((16, ROW))
    for w in range(4):
        d = np.zeros((4, 4))
        for k in range(4):
            for m in range(k + 1):
                d[m][k - m] += C[w, k] * comb(k, m)
        for m in range(4):
            D[4 * m + w] = sum(d[m][k] * u ** k for k in range(4))
    D = D.astype(np.float32)
    D1 = np.zeros((32, 2 * ROW), np.float32)
    D1[0:16, 0:ROW] = D
    D1[16:32, ROW:2 * ROW] = D
    D2 = np.zeros((64, ROW), np.float32)
    D2[0:16] = D; D2[16:32] = D; D2[32:48] = D; D2[48:64] = D

    tiles = plan["tiles"]
    ntiles = len(tiles)
    npairs = ntiles // 2
    nrows = plan["nrows"]
    phi_pad = np.concatenate([plan["phi_row"], np.zeros(300)])
    cg_pad = np.concatenate([plan["cross_g"], np.full(300, ROW)])
    PHIQ1 = np.zeros((npairs, 32, 128), np.float32)
    PHIQ2 = np.zeros((npairs, 64, 128), np.float32)
    gstar = np.zeros((128, ntiles), np.float32)
    for pi in range(npairs):
        for half in range(2):
            r0, nr, j0, dummy = tiles[2 * pi + half]
            ph = phi_pad[r0:r0 + 128]
            gstar[:, 2 * pi + half] = cg_pad[r0:r0 + 128]
            for m in range(4):
                pm = (ph ** m).astype(np.float32)
                qa = (-(ph ** m)).astype(np.float32)
                qb = ((ph - 1.0) ** m).astype(np.float32)
                for w in range(4):
                    PHIQ1[pi, 16 * half + 4 * m + w] = pm
                    PHIQ2[pi, 32 * half + 4 * m + w] = qa
                    PHIQ2[pi, 32 * half + 16 + 4 * m + w] = qb
    grow = np.broadcast_to(np.arange(ROW, dtype=np.float32), (128, ROW)).copy()
    return dict(D1=D1, D2=D2, PHIQ1=PHIQ1, PHIQ2=PHIQ2, gstar=gstar,
                grow=grow, sigma=sigma, npairs=npairs)


def make_cpx(cp):
    raise NotImplementedError


class Ctr:
    def __init__(self):
        self.v = {}

    def key(self, sem):
        return getattr(sem, "name", None) or id(sem)

    def inc(self, sem, n=1):
        self.v[self.key(sem)] = self.get(sem) + n

    def get(self, sem):
        return self.v.get(self.key(sem), 0)


def alloc_recon_bufs(nc, Ws, tbl, pb=None, arena=None):
    b = {}
    if arena is None:
        def mk(name, shape, dt):
            return mk(name, shape, dt)
    else:
        arF = arena.bitcast(F32)
        arB = arena
        state = {"byte": 0}

        def mk(name, shape, dt):
            P, C = shape
            b0 = (state["byte"] + 3) & ~3
            w = 4 if dt == F32 else 2
            state["byte"] = b0 + C * w
            if dt == F32:
                return arF[0:P, b0 // 4:b0 // 4 + C]
            return arB[0:P, b0 // 2:b0 // 2 + C]
    b["phiq1_dram"] = nc.dram_tensor("phiq1_t", list(tbl["PHIQ1"].shape), F32,
                                     kind="ExternalInput").ap()
    b["phiq2_dram"] = nc.dram_tensor("phiq2_t", list(tbl["PHIQ2"].shape), F32,
                                     kind="ExternalInput").ap()
    b["gstar_dram"] = nc.dram_tensor("gstar_t", list(tbl["gstar"].shape), F32,
                                     kind="ExternalInput").ap()
    b["grow_dram"] = nc.dram_tensor("grow_t", [128, ROW], F32,
                                    kind="ExternalInput").ap()
    b["d1_dram"] = nc.dram_tensor("d1_t", [32, 2 * ROW], F32,
                                  kind="ExternalInput").ap()
    b["d2_dram"] = nc.dram_tensor("d2_t", [64, ROW], F32,
                                  kind="ExternalInput").ap()
    b["s_phiq1"] = [mk(f"s_phiq1_{p}", [32, 128], F32)
                    for p in range(2)]
    b["s_phiq2"] = [mk(f"s_phiq2_{p}", [64, 128], F32)
                    for p in range(2)]
    b["s_cpw1"] = [{W: mk(f"s_cpw1_{p}_{W}", [32, 128], F32)
                    for W in Ws} for p in range(2)]
    b["s_cpw2"] = [{W: mk(f"s_cpw2_{p}_{W}", [64, 128], F32)
                    for W in Ws} for p in range(2)]
    b["s_a1"] = [{W: mk(f"s_a1_{p}_{W}", [32, 128], BF16)
                  for W in Ws} for p in range(2)]
    b["s_a2"] = [{W: mk(f"s_a2_{p}_{W}", [64, 128], BF16)
                  for W in Ws} for p in range(2)]
    b["s_d1"] = mk("s_d1", [32, 2 * ROW], F32)
    b["s_d2"] = mk("s_d2", [64, ROW], F32)
    b["s_d1b"] = mk("s_d1b", [32, 2 * ROW], BF16)
    b["s_d2b"] = mk("s_d2b", [64, ROW], BF16)
    b["s_grow"] = mk("s_grow", [128, ROW], F32)
    ntl = tbl["gstar"].shape[1]
    b["s_gstar"] = mk("s_gstar", [128, ntl], F32)
    b["s_mask"] = [mk(f"s_mask{p}", [128, 2 * ROW], F32)
                   for p in range(2)]
    b["s_t"] = mk("s_t", [128, 2 * ROW], F32)
    b["s_out"] = [{W: mk(f"s_out_{p}_{W}", [128, 2 * ROW], BF16)
                   for W in Ws} for p in range(2)]
    if pb is None:
        pb = [nc.alloc_psum_tensor(f"pbank{k}", [128, 512], F32)
              for k in range(8)]
    b["pbanks"] = pb
    b["p1"] = [{W: pb[p * 3 + wi][:, 0:258] for wi, W in enumerate(Ws)}
               for p in range(2)]
    b["p2a"] = [{W: pb[p * 3 + wi][:, 258:387] for wi, W in enumerate(Ws)}
                for p in range(2)]
    b["p2b"] = [{W: pb[6 + p][:, wi * 129:(wi + 1) * 129]
                 for wi, W in enumerate(Ws)} for p in range(2)]
    return b


def emit_recon_prologue(eng, bufs, sems, ctr):
    sync, vector = eng["sync"], eng["vector"]
    ldt, ab = sems["r_ldt"], sems["r_ab"]
    sync.dma_start(out=bufs["s_grow"][:, :], in_=bufs["grow_dram"]).then_inc(ldt, 16)
    sync.dma_start(out=bufs["s_gstar"][:, :], in_=bufs["gstar_dram"]).then_inc(ldt, 16)
    sync.dma_start(out=bufs["s_d1"][:, :], in_=bufs["d1_dram"]).then_inc(ldt, 16)
    sync.dma_start(out=bufs["s_d2"][:, :], in_=bufs["d2_dram"]).then_inc(ldt, 16)
    ctr.inc(ldt, 64)
    vector.wait_ge(ldt, ctr.get(ldt))
    vector.tensor_copy(bufs["s_d1b"][:, :], bufs["s_d1"][:, :])
    vector.tensor_copy(bufs["s_d2b"][:, :], bufs["s_d2"][:, :]).then_inc(ab, 1)
    ctr.inc(ab, 1)


def emit_recon(eng, bufs, plan, tbl, cp_aps, out_aps, sems, ctr,
               max_pairs=None):
    sync, vector, gpsimd, tensor = (eng["sync"], eng["vector"],
                                    eng["gpsimd"], eng["tensor"])
    tiles = plan["tiles"]
    npairs = tbl["npairs"] if max_pairs is None else min(tbl["npairs"], max_pairs)
    Ws = list(cp_aps.keys())
    NW = len(Ws)
    ab, mm, bl = sems["r_ab"], sems["r_mm"], sems["r_bl"]
    lds = [sems["r_ld0"], sems["r_ld1"]]
    sts = [sems["r_st0"], sems["r_st1"]]
    st_tot = []

    for pi in range(npairs):
        par = pi % 2
        r0a, nra, j0a, duma = tiles[2 * pi]
        r0b, nrb, j0b, dumb = tiles[2 * pi + 1]
        dj = j0b - j0a

        ld = lds[par]
        st = sts[par]
        scalarq = eng["scalar"]
        if pi == 0:
            sync.wait_ge(sems["r_cpx"], ctr.get(sems["r_cpx"]))
            scalarq.wait_ge(sems["r_cpx"], ctr.get(sems["r_cpx"]))
        if pi >= 2:
            sync.wait_ge(ab, 1 + (pi - 1))
            scalarq.wait_ge(ab, 1 + (pi - 1))
        sync.dma_start(out=bufs["s_phiq1"][par][:, :],
                       in_=bufs["phiq1_dram"][pi, :, :]).then_inc(ld, 16)
        sync.dma_start(out=bufs["s_phiq2"][par][:, :],
                       in_=bufs["phiq2_dram"][pi, :, :]).then_inc(ld, 16)
        ctr.inc(ld, 32)
        for wi, W in enumerate(Ws):
            q = sync if wi == 0 else scalarq
            cpx = cp_aps[W]
            for half, j0 in enumerate([j0a, j0b]):
                win = bass.AP(tensor=cpx.tensor, offset=cpx.offset + j0,
                              ap=[[0, 4], [1, 4], [1, 128]])
                q.dma_start(
                    out=bufs["s_cpw1"][par][W][16 * half:16 * half + 16, :],
                    in_=win).then_inc(ld, 16)
                winB = bass.AP(tensor=cpx.tensor, offset=cpx.offset + j0 + 1,
                               ap=[[0, 4], [1, 4], [1, 128]])
                q.dma_start(
                    out=bufs["s_cpw2"][par][W][32 * half:32 * half + 16, :],
                    in_=win).then_inc(ld, 16)
                q.dma_start(
                    out=bufs["s_cpw2"][par][W][32 * half + 16:32 * half + 32, :],
                    in_=winB).then_inc(ld, 16)
                ctr.inc(ld, 48)

        gpsimd.wait_ge(ld, ctr.get(ld))
        if pi >= 2:
            gpsimd.wait_ge(mm, pi - 1)
        for wi, W in enumerate(Ws):
            gpsimd.tensor_tensor(bufs["s_a1"][par][W][:, :],
                                 bufs["s_cpw1"][par][W][:, :],
                                 bufs["s_phiq1"][par][:, :], ALU.mult)
            ins = gpsimd.tensor_tensor(bufs["s_a2"][par][W][:, :],
                                       bufs["s_cpw2"][par][W][:, :],
                                       bufs["s_phiq2"][par][:, :], ALU.mult)
            if wi == NW - 1:
                ins.then_inc(ab, 1)
        ctr.inc(ab, 1)

        tensor.wait_ge(ab, ctr.get(ab))
        if pi >= 2:
            tensor.wait_ge(bl, pi - 1)
        for wi, W in enumerate(Ws):
            tensor.matmul(bufs["p1"][par][W], bufs["s_a1"][par][W][0:32, :],
                          bufs["s_d1b"][:, :], start=True, stop=True,
                          skip_group_check=True)
            tensor.matmul(bufs["p2a"][par][W], bufs["s_a2"][par][W][0:32, :],
                          bufs["s_d2b"][0:32, :], start=True, stop=True,
                          skip_group_check=True)
            ins = tensor.matmul(bufs["p2b"][par][W], bufs["s_a2"][par][W][32:64, :],
                                bufs["s_d2b"][32:64, :], start=True, stop=True,
                                skip_group_check=True)
            if wi == NW - 1:
                ins.then_inc(mm, 1)
        ctr.inc(mm, 1)

        vector.wait_ge(mm, ctr.get(mm))
        if pi >= 2:
            vector.wait_ge(st, st_tot[pi - 2])
        mk = bufs["s_mask"][par]
        vector.tensor_scalar(mk[:, 0:ROW], bufs["s_grow"][:, :],
                             bufs["s_gstar"][:, 2 * pi:2 * pi + 1], None, ALU.is_ge)
        vector.tensor_scalar(mk[:, ROW:2 * ROW], bufs["s_grow"][:, :],
                             bufs["s_gstar"][:, 2 * pi + 1:2 * pi + 2], None,
                             ALU.is_ge)
        for wi, W in enumerate(Ws):
            t_s = bufs["s_t"]
            vector.tensor_tensor(t_s[:, 0:ROW], mk[:, 0:ROW],
                                 bufs["p2a"][par][W], ALU.mult)
            vector.tensor_tensor(t_s[:, ROW:2 * ROW], mk[:, ROW:2 * ROW],
                                 bufs["p2b"][par][W], ALU.mult)
            ins = vector.tensor_add(bufs["s_out"][par][W][:, :], t_s[:, :],
                                    bufs["p1"][par][W])
            if wi == NW - 1:
                ins.then_inc(bl, 1)
        ctr.inc(bl, 1)

        sync.wait_ge(bl, ctr.get(bl))
        for W in Ws:
            out = out_aps[W]
            for half, (r0, nr, dum) in enumerate(
                    [(r0a, nra, duma), (r0b, nrb, dumb)]):
                if dum:
                    continue
                src = bufs["s_out"][par][W][0:nr, half * ROW:(half + 1) * ROW]
                dst = bass.AP(tensor=out.tensor, offset=out.offset + r0 * ROW,
                              ap=[[ROW, nr], [1, ROW]])
                sync.dma_start(out=dst, in_=src).then_inc(st, 16)
                ctr.inc(st, 16)
        st_tot.append(ctr.get(st))

    sync.wait_ge(st, ctr.get(st))


CPX_PAD = 140


def emit_cpx_build(eng, cp_ap, cpx_ap, n_ctrl, sems, ctr):
    sync = eng["sync"]
    ld = sems["r_cpx"]
    sync.dma_start(out=cpx_ap[1:n_ctrl + 1], in_=cp_ap[:]).then_inc(ld, 16)
    sync.dma_start(out=cpx_ap[0:1], in_=cp_ap[0:1]).then_inc(ld, 16)
    npad = CPX_PAD - 1
    last = bass.AP(tensor=cp_ap.tensor, offset=cp_ap.offset + n_ctrl - 1,
                   ap=[[0, npad], [1, 1]])
    sync.dma_start(out=cpx_ap[n_ctrl + 1:n_ctrl + CPX_PAD]
                   .rearrange('(a b) -> a b', b=1), in_=last).then_inc(ld, 16)
    ctr.inc(ld, 48)


class _RecCall:
    def __init__(self, entry):
        self.entry = entry

    def then_inc(self, sem, n=1):
        self.entry[3].append((sem, n))
        return self


class _RecEngine:
    def __init__(self, script):
        self.script = script

    def __getattr__(self, meth):
        def call(*args, **kw):
            entry = [meth, args, kw, []]
            self.script.append(entry)
            return _RecCall(entry)
        return call


class Recorder:
    ENGINES = ("sync", "vector", "gpsimd", "tensor", "scalar")

    def __init__(self):
        self.scripts = {e: [] for e in self.ENGINES}

    def eng(self):
        return {e: _RecEngine(self.scripts[e]) for e in self.ENGINES}

    def replay(self, name, engine):
        for meth, args, kw, incs in self.scripts[name]:
            ins = getattr(engine, meth)(*args, **kw)
            for sem, n in incs:
                ins = ins.then_inc(sem, n)



D = 2048
R = 2048
DH = 512
INTER = 8192
OWN = 1024
PRE = 3072
NTOK = PRE + 2 + OWN
EPS = 1e-5

MAIN_SEMS = ["m_ldx0", "m_ldx1", "m_ldw0", "m_ldw1", "m_ldc", "m_st",
             "m_st0", "m_st1", "m_v", "m_a", "m_p", "m_t"]


def alloc_main(nc, v0_external_wbf=True):
    b = {}
    d = b
    d["x_own"] = nc.dram_tensor("x_own", [OWN + 2, D], F32, kind="ExternalInput").ap()
    d["x_pre"] = nc.dram_tensor("x_pre", [PRE, D], F32, kind="ExternalInput").ap()
    d["premask"] = nc.dram_tensor("premask", [128, 48], F32, kind="ExternalInput").ap()
    d["Wu"] = nc.dram_tensor("Wu", [R, D], F32, kind="ExternalInput").ap()
    d["Wv"] = nc.dram_tensor("Wv", [R, D], F32, kind="ExternalInput").ap()
    d["Wg1"] = nc.dram_tensor("Wg1", [DH, D], F32, kind="ExternalInput").ap()
    d["Wg2"] = nc.dram_tensor("Wg2", [R, DH], F32, kind="ExternalInput").ap()
    d["biases"] = nc.dram_tensor("biases", [128, 5 * 16], F32, kind="ExternalInput").ap()
    d["conv_w"] = nc.dram_tensor("conv_w", [128, 16 * 3], F32, kind="ExternalInput").ap()
    kind = "ExternalInput" if v0_external_wbf else "Internal"
    d["gate_bf"] = nc.dram_tensor("gate_bf", [INTER * D + 256], BF16, kind=kind).ap()
    d["up_bf"] = nc.dram_tensor("up_bf", [INTER * D + 256], BF16, kind=kind).ap()
    d["down_bf"] = nc.dram_tensor("down_bf", [INTER * D + 256], BF16, kind=kind).ap()
    d["out"] = nc.dram_tensor("out", [OWN, D], F32, kind="ExternalOutput").ap()

    d["Wu_bf"] = nc.dram_tensor("Wu_bf", [R, D], BF16).ap()
    d["Wv_bf"] = nc.dram_tensor("Wv_bf", [R, D], BF16).ap()
    d["Wg1_bf"] = nc.dram_tensor("Wg1_bf", [DH, D], BF16).ap()
    d["Wg2_bf"] = nc.dram_tensor("Wg2_bf", [R, DH], BF16).ap()
    d["WuT_pk"] = nc.dram_tensor("WuT_pk", [4, 16, 128, 512], BF16).ap()
    d["WvT_pk"] = nc.dram_tensor("WvT_pk", [4, 16, 128, 512], BF16).ap()
    d["WuT_dram"] = nc.dram_tensor("WuT_dram", [R, D], BF16).ap()
    d["xn_dram"] = nc.dram_tensor("xn_dram", [NTOK + 32, D], BF16).ap()
    d["tout_dram"] = nc.dram_tensor("tout_dram", [D * OWN], BF16).ap()
    d["xmid_dram"] = nc.dram_tensor("xmid_dram", [OWN, D], F32).ap()
    d["hn_dram"] = nc.dram_tensor("hn_dram", [OWN, D], BF16).ap()
    d["gwT_pk"] = nc.dram_tensor("gwT_pk", [16, 16, 128, 512], BF16).ap()
    d["upT_pk"] = nc.dram_tensor("upT_pk", [16, 16, 128, 512], BF16).ap()
    d["dwT"] = nc.dram_tensor("dwT", [INTER * D], BF16).ap()

    sb = nc.alloc_sbuf_tensor
    A = sb("arenaA", [128, 16384], F32)
    A16 = A.ap().bitcast(BF16)
    A32 = A.ap()
    Bar = sb("arenaB", [128, 8192], BF16).ap()
    Car = sb("arenaC", [128, 16896], BF16).ap()
    Ear = sb("arenaE", [128, 16384], BF16).ap()
    d["s_x"] = [A32[:, 0:2048], A32[:, 2048:4096]]
    d["s_sqv"] = A32[:, 4096:6144]
    d["s_sqa"] = A32[:, 6144:8192]
    d["s_xn"] = [A16[:, 16384:18432], A16[:, 18432:20480]]
    d["s_wupk"] = [A16[:, 0:8192], A16[:, 8192:16384]]
    d["s_wv"] = [A16[:, 16384:24576], A16[:, 24576:32768]]
    d["s_wun"] = [A16[:, 0:8192], A16[:, 8192:16384]]
    d["s_inter"] = A16[:, 0:32768]
    d["s_wtr"] = Bar[:, 0:8192]
    d["s_go"] = Bar[:, 0:8192]
    d["s_xnf"] = [Car[:, 0:8448], Car[:, 8448:16896]]
    d["_Car"] = Car
    d["s_gw"] = [Car[:, 0:8192], Car[:, 8192:16384]]
    d["s_wg1t"] = Ear[:, 0:8192]
    d["s_wg2t"] = Ear[:, 8192:16384]
    d["s_uw"] = [Ear[:, 0:8192], Ear[:, 8192:16384]]
    d["s_hnf"] = sb("s_hnf", [128, 16 * 512], BF16)
    d["s_lncol"] = sb("s_lncol", [128, 32], F32)
    d["s_premask"] = sb("s_premask", [128, 48], F32)
    d["s_colsum"] = sb("s_colsum", [128, 96], F32)
    d["s_off"] = sb("s_off", [128, 16], F32)
    d["s_carry"] = sb("s_carry", [128, 16], F32)
    d["s_bias"] = sb("s_bias", [128, 5 * 16], F32)
    d["s_convw"] = sb("s_convw", [128, 16 * 3], F32)
    d["s_usb"] = sb("s_usb", [128, 512], F32)
    d["s_gsb"] = sb("s_gsb", [128, 512], F32)
    d["s_vsb"] = sb("s_vsb", [128, 512], F32)
    d["s_ug"] = sb("s_ug", [128, 512], F32)
    d["s_vg"] = sb("s_vg", [128, 512], BF16)
    d["s_kc"] = sb("s_kc", [128, 512], F32)
    d["s_hf"] = sb("s_hf", [128, 4 * 512], BF16)
    d["s_hraw"] = sb("s_hraw", [128, 512], F32)
    d["s_ev"] = [sb(f"s_ev{i}", [128, 512], F32) for i in range(2)]
    d["s_tout"] = [sb(f"s_tout{i}", [128, 512], BF16) for i in range(2)]
    d["s_toutk"] = [sb(f"s_toutk{i}", [128, 512], BF16) for i in range(2)]
    d["s_xsl"] = [sb(f"s_xsl{i}", [128, 512], F32) for i in range(2)]
    d["s_dw"] = [sb(f"s_dw{i}", [128, D], BF16) for i in range(2)]
    d["s_silu"] = sb("s_silu", [128, 1024], F32)
    d["s_o"] = [sb(f"s_o{i}", [128, 512], F32) for i in range(2)]
    d["pb"] = [nc.alloc_psum_tensor(f"mpb{k}", [128, 512], F32) for k in range(8)]
    return b


def prologue(eng, b, sems, ctr, wbf_ready=0):
    sync, scalar, gpsimd = eng["sync"], eng["scalar"], eng["gpsimd"]
    ldc, ldw, st, g = sems["m_ldc"], sems["m_ldw0"], sems["m_st"], sems["m_p"]
    sync.dma_start(out=b["s_bias"][:, :], in_=b["biases"]).then_inc(ldc, 16)
    sync.dma_start(out=b["s_convw"][:, :], in_=b["conv_w"]).then_inc(ldc, 16)
    sync.dma_start(out=b["s_premask"][:, :], in_=b["premask"]).then_inc(ldc, 16)
    ctr.inc(ldc, 48)
    for w, wb in [("Wu", "Wu_bf"), ("Wv", "Wv_bf"), ("Wg1", "Wg1_bf"),
                  ("Wg2", "Wg2_bf")]:
        gpsimd.dma_start(out=b[wb], in_=b[w]).then_inc(g, 16)
        ctr.inc(g, 16)
    pdone = ctr.get(g)
    sync.wait_ge(g, pdone)

    def tr_store(in_ap, stage_cols, dst_ap):
        sync.wait_ge(st, ctr.get(st))
        sync.dma_start(out=b["s_wtr"][:, 0:stage_cols], in_=in_ap,
                       transpose=True).then_inc(ldw, 16)
        ctr.inc(ldw, 16)
        scalar.wait_ge(ldw, ctr.get(ldw))
        src = b["s_wtr"][:, 0:stage_cols]
        scalar.dma_start(out=dst_ap, in_=src).then_inc(st, 16)
        ctr.inc(st, 16)

    for k in range(16):
        sync.dma_start(out=b["s_wg1t"][:, k * DH:(k + 1) * DH],
                       in_=b["Wg1_bf"][0:DH, 128 * k:128 * k + 128],
                       transpose=True).then_inc(ldw, 16)
        ctr.inc(ldw, 16)
    for k in range(4):
        sync.dma_start(out=b["s_wg2t"][:, k * R:(k + 1) * R],
                       in_=b["Wg2_bf"][0:R, 128 * k:128 * k + 128],
                       transpose=True).then_inc(ldw, 16)
        ctr.inc(ldw, 16)
    for w, pk in [("Wu_bf", "WuT_pk"), ("Wv_bf", "WvT_pk")]:
        for k in range(16):
            dst = bass.AP(tensor=b[pk].tensor, offset=k * (128 * 512),
                          ap=[[512, 128], [16 * 128 * 512, 4], [1, 512]])
            sync.wait_ge(st, ctr.get(st))
            sync.dma_start(out=b["s_wtr"][:, 0:R],
                           in_=b[w][0:R, 128 * k:128 * k + 128],
                           transpose=True).then_inc(ldw, 16)
            ctr.inc(ldw, 16)
            scalar.wait_ge(ldw, ctr.get(ldw))
            src = b["s_wtr"][:, 0:R].rearrange("p (g c) -> p g c", g=4)
            scalar.dma_start(out=dst, in_=src).then_inc(st, 16)
            ctr.inc(st, 16)
    for k in range(16):
        tr_store(b["Wu_bf"][0:D, 128 * k:128 * k + 128], D,
                 b["WuT_dram"][128 * k:128 * k + 128, :])
    if True:
        gw = b["gate_bf"][0:INTER * D].rearrange("(i h) -> i h", h=D)
        uw = b["up_bf"][0:INTER * D].rearrange("(i h) -> i h", h=D)
        dw = b["down_bf"][0:INTER * D].rearrange("(h i) -> h i", i=INTER)
        dwt = b["dwT"].rearrange("(i h) -> i h", h=D)
        for wsrc, pk in [(gw, "gwT_pk"), (uw, "upT_pk")]:
            for k in range(16):
                dst = bass.AP(tensor=b[pk].tensor, offset=k * (128 * 512),
                              ap=[[512, 128], [16 * 128 * 512, 16], [1, 512]])
                sync.wait_ge(st, ctr.get(st))
                sync.dma_start(out=b["s_wtr"][:, :],
                               in_=wsrc[0:INTER, 128 * k:128 * k + 128],
                               transpose=True).then_inc(ldw, 16)
                ctr.inc(ldw, 16)
                scalar.wait_ge(ldw, ctr.get(ldw))
                src = b["s_wtr"][:, :].rearrange("p (g c) -> p g c", g=16)
                scalar.dma_start(out=dst, in_=src).then_inc(st, 16)
                ctr.inc(st, 16)
        for k in range(64):
            tr_store(dw[0:D, 128 * k:128 * k + 128], D,
                     dwt[128 * k:128 * k + 128, :])


def ln_stream(eng, b, sems, ctr, srcs, xn_dst, gate_sts=False):
    sync, scalar, vector = eng["sync"], eng["scalar"], eng["vector"]
    pool = eng["gpsimd"]
    v, a, p = sems["m_v"], sems["m_a"], sems["m_p"]
    if gate_sts:
        for snm in ["m_st", "m_st0", "m_st1"]:
            sync.wait_ge(sems[snm], ctr.get(sems[snm]))
    scalar.wait_ge(sems["m_ldc"], ctr.get(sems["m_ldc"]))
    tiles = []
    for ap, n, drow in srcs:
        r = 0
        while r < n:
            nr = min(128, n - r)
            tiles.append((ap, r, nr, drow + r))
            r += nr
    vmark, stmark = {}, {}
    for t, (ap, r0, nr, orow) in enumerate(tiles):
        par = t % 2
        ldx = sems[f"m_ldx{par}"]
        st = sems[f"m_st{par}"]
        sx, sxn, col = b["s_x"][par], b["s_xn"][par], b["s_lncol"]
        cb = 16 * par
        c = lambda k: col[0:nr, cb + k:cb + k + 1]
        if t >= 2:
            sync.wait_ge(v, vmark[t - 2])
        sync.dma_start(out=sx[0:nr, :], in_=ap[r0:r0 + nr, :]).then_inc(ldx, 16)
        ctr.inc(ldx, 16)
        ldn = ctr.get(ldx)
        scalar.wait_ge(ldx, ldn)
        scalar.activation(b["s_sqv"][0:nr, :], sx[0:nr, :], AF.Identity,
                          accum_out=c(0))
        scalar.activation(b["s_sqa"][0:nr, :], sx[0:nr, :], AF.Square,
                          accum_out=c(1)).then_inc(a, 1)
        ctr.inc(a, 1)
        vector.wait_ge(a, ctr.get(a))
        vector.tensor_scalar(c(2), c(0), -1.0 / D, None, ALU.mult)
        vector.tensor_scalar(c(5), c(1), 1.0 / D, EPS, ALU.mult,
                             ALU.add).then_inc(v, 1)
        ctr.inc(v, 1)
        scalar.wait_ge(v, ctr.get(v))
        scalar.activation(c(3), c(2), AF.Square).then_inc(a, 1)
        ctr.inc(a, 1)
        pool.wait_ge(a, ctr.get(a))
        pool.tensor_tensor(c(4), c(5), c(3), ALU.subtract).then_inc(p, 1)
        ctr.inc(p, 1)
        scalar.wait_ge(p, ctr.get(p))
        scalar.activation(c(6), c(4), AF.Ln).then_inc(a, 1)
        ctr.inc(a, 1)
        vector.wait_ge(a, ctr.get(a))
        vector.tensor_scalar(c(7), c(6), -0.5, None, ALU.mult).then_inc(v, 1)
        ctr.inc(v, 1)
        scalar.wait_ge(v, ctr.get(v))
        scalar.activation(c(8), c(7), AF.Exp).then_inc(a, 1)
        ctr.inc(a, 1)
        pool.wait_ge(a, ctr.get(a))
        pool.tensor_tensor(c(9), c(2), c(8), ALU.mult).then_inc(p, 1)
        ctr.inc(p, 1)
        vector.wait_ge(p, ctr.get(p))
        if t >= 2:
            vector.wait_ge(st, stmark[t - 2])
        vector.tensor_scalar(sxn[0:nr, :], sx[0:nr, :], c(8),
                             c(9), ALU.mult, ALU.add).then_inc(v, 1)
        ctr.inc(v, 1)
        vmark[t] = ctr.get(v)
        scalar.wait_ge(v, ctr.get(v))
        scalar.dma_start(out=xn_dst[orow:orow + nr, :],
                         in_=sxn[0:nr, :]).then_inc(st, 16)
        ctr.inc(st, 16)
        stmark[t] = ctr.get(st)


def attn_phase(eng, b, sems, ctr, cut=3, subcut=99, dbg=None):
    sync, scalar, vector, tensor = (eng["sync"], eng["scalar"], eng["vector"],
                                    eng["tensor"])
    v, a, t_s = sems["m_v"], sems["m_a"], sems["m_t"]
    pb = b["pb"]
    for snm in ["m_st", "m_st0", "m_st1"]:
        sync.wait_ge(sems[snm], ctr.get(sems[snm]))
    scalar.wait_ge(sems["m_ldc"], ctr.get(sems["m_ldc"]))
    vector.wait_ge(sems["m_ldc"], ctr.get(sems["m_ldc"]))
    toutd = b["tout_dram"].rearrange("(d t) -> d t", t=OWN)
    marks = {"uvdone": 0}
    if cut <= 0:
        return

    def do_ttile(ti, own):
        par = ti % 2
        ldx = sems[f"m_ldx{par}"]
        xnf = b["s_xnf"][par]
        if own:
            row0 = PRE + (ti - 6) * 512
            width = 528
        else:
            row0 = ti * 512
            width = 512
        if ti >= 2:
            sync.wait_ge(t_s, marks[("tdone", ti - 2)])
        for k in range(16):
            sync.dma_start(out=xnf[:, k * 528:k * 528 + width],
                           in_=b["xn_dram"][row0:row0 + width,
                                            128 * k:128 * k + 128],
                           transpose=True).then_inc(ldx, 16)
            ctr.inc(ldx, 16)
        xn_ld = ctr.get(ldx)
        off = 1 if own else 0
        if subcut <= 11:
            marks[("tdone", ti)] = ctr.get(t_s)
            return

        if own:
            for k in range(16):
                tp = k % 2
                stp = sems[f"m_st{tp}"]
                vector.wait_ge(ldx, xn_ld)
                if k >= 2:
                    vector.wait_ge(stp, marks[("tst", ti, k - 2)])
                cw = b["s_convw"]
                vector.tensor_scalar(b["s_ev"][0][:, :],
                                     xnf[:, k * 528:k * 528 + 512],
                                     cw[:, 3 * k:3 * k + 1],
                                     b["s_bias"][:, 48 + k:48 + k + 1],
                                     ALU.mult, ALU.add)
                vector.scalar_tensor_tensor(b["s_ev"][1][:, :],
                                            xnf[:, k * 528 + 1:k * 528 + 513],
                                            cw[:, 3 * k + 1:3 * k + 2],
                                            b["s_ev"][0][:, :], ALU.mult, ALU.add)
                vector.scalar_tensor_tensor(b["s_tout"][tp][:, :],
                                            xnf[:, k * 528 + 2:k * 528 + 514],
                                            cw[:, 3 * k + 2:3 * k + 3],
                                            b["s_ev"][1][:, :], ALU.mult,
                                            ALU.add).then_inc(v, 1)
                ctr.inc(v, 1)
                scalar.wait_ge(v, ctr.get(v))
                scalar.dma_start(
                    out=toutd[128 * k:128 * k + 128,
                              (ti - 6) * 512:(ti - 6) * 512 + 512],
                    in_=b["s_tout"][tp][:, :]).then_inc(stp, 16)
                ctr.inc(stp, 16)
                marks[("tst", ti, k)] = ctr.get(stp)

        tensor.wait_ge(ldx, xn_ld)
        tensor.wait_ge(v, ctr.get(v))
        if ti >= 7:
            tensor.wait_ge(v, marks["uvdone"])
        for m in range(4):
            for k in range(16):
                mi = tensor.matmul(pb[4 + m][:, :],
                                   b["s_wg1t"][:, k * DH + m * 128:k * DH + m * 128 + 128],
                                   xnf[:, k * 528 + off:k * 528 + off + 512],
                                   start=(k == 0), stop=(k == 15),
                                   skip_group_check=True)
        mi.then_inc(t_s, 1)
        ctr.inc(t_s, 1)
        if subcut <= 12:
            marks[("tdone", ti)] = ctr.get(t_s)
            return
        scalar.wait_ge(t_s, ctr.get(t_s))
        for m in range(4):
            scalar.activation(b["s_hraw"][:, :], pb[4 + m][:, :], AF.Identity,
                              bias=b["s_bias"][:, 64 + m:64 + m + 1])
            ins = scalar.activation(b["s_hf"][:, m * 512:(m + 1) * 512],
                                    b["s_hraw"][:, :], AF.Relu)
        ins.then_inc(a, 1)
        ctr.inc(a, 1)
        a_h = ctr.get(a)
        if subcut <= 13:
            tensor.wait_ge(a, a_h)
            marks[("tdone", ti)] = ctr.get(t_s)
            return

        for grp in range(4):
            wpar = grp % 2
            ldw = sems[f"m_ldw{wpar}"]
            wu = b["s_wupk"][wpar]
            wv = b["s_wv"][wpar]
            if grp >= 2:
                sync.wait_ge(t_s, marks[("gdone", ti, grp - 2)])
            elif ti == 7:
                sync.wait_ge(v, marks["uvdone"])
            elif ti >= 1:
                sync.wait_ge(t_s, marks[("gdone", ti - 1, grp + 2)])
            src = bass.AP(tensor=b["WuT_pk"].tensor, offset=grp * (16 * 128 * 512),
                          ap=[[512, 128], [128 * 512, 16], [1, 512]])
            sync.dma_start(out=wu[:, :], in_=src).then_inc(ldw, 16)
            ctr.inc(ldw, 16)
            if own:
                srcv = bass.AP(tensor=b["WvT_pk"].tensor,
                               offset=grp * (16 * 128 * 512),
                               ap=[[512, 128], [128 * 512, 16], [1, 512]])
                sync.dma_start(out=wv[:, :], in_=srcv).then_inc(ldw, 16)
                ctr.inc(ldw, 16)
            w_ld = ctr.get(ldw)
            tensor.wait_ge(ldw, w_ld)
            for j in range(4):
                r = grp * 4 + j
                for k in range(16):
                    tensor.matmul(pb[j][:, :],
                                  wu[:, k * 512 + j * 128:k * 512 + j * 128 + 128],
                                  xnf[:, k * 528 + off:k * 528 + off + 512],
                                  start=(k == 0), stop=(k == 15),
                                  skip_group_check=True)
                if own:
                    tensor.wait_ge(a, a_h)
                    if not (grp == 0 and j == 0):
                        tensor.wait_ge(v, ctr.get(v))
                    for k in range(16):
                        tensor.matmul(pb[4][:, :],
                                      wv[:, k * 512 + j * 128:k * 512 + j * 128 + 128],
                                      xnf[:, k * 528 + off:k * 528 + off + 512],
                                      start=(k == 0), stop=(k == 15),
                                      skip_group_check=True)
                tensor.wait_ge(a, a_h)
                for m in range(4):
                    mi = tensor.matmul(pb[5][:, :],
                                       b["s_wg2t"][:, m * R + r * 128:m * R + r * 128 + 128],
                                       b["s_hf"][:, m * 512:(m + 1) * 512],
                                       start=(m == 0), stop=(m == 3),
                                       skip_group_check=True)
                mi.then_inc(t_s, 1)
                ctr.inc(t_s, 1)
                scalar.wait_ge(t_s, ctr.get(t_s))
                scalar.activation(b["s_usb"][:, :], pb[j][:, :], AF.Identity,
                                  bias=b["s_bias"][:, r:r + 1])
                if own:
                    scalar.activation(b["s_vsb"][:, :], pb[4][:, :],
                                      AF.Identity,
                                      bias=b["s_bias"][:, 16 + r:16 + r + 1])
                scalar.activation(b["s_gsb"][:, :], pb[5][:, :], AF.Sigmoid,
                                  bias=b["s_bias"][:, 32 + r:32 + r + 1]
                                  ).then_inc(a, 1)
                ctr.inc(a, 1)
                if subcut <= 14:
                    tensor.wait_ge(a, ctr.get(a))
                    continue
                vector.wait_ge(a, ctr.get(a))
                if not own:
                    blk = ti // 2
                    half = ti % 2
                    cs = b["s_colsum"][:, half * 48 + blk * 16 + r:
                                       half * 48 + blk * 16 + r + 1]
                    vector.scalar_tensor_tensor(
                        b["s_ug"][:, :], b["s_usb"][:, :], 1.0, b["s_gsb"][:, :],
                        ALU.mult, ALU.mult, accum_out=cs).then_inc(v, 1)
                else:
                    vector.scalar_tensor_tensor(
                        b["s_ug"][:, :], b["s_usb"][:, :], 1.0, b["s_gsb"][:, :],
                        ALU.mult, ALU.mult)
                    vector.tensor_tensor(b["s_vg"][:, :], b["s_vsb"][:, :],
                                         b["s_gsb"][:, :], ALU.mult)
                    init = (b["s_off"][:, r:r + 1] if ti == 6
                            else b["s_carry"][:, r:r + 1])
                    vector.tensor_tensor_scan(b["s_kc"][:, :], b["s_ug"][:, :],
                                              b["s_ug"][:, :], init,
                                              ALU.add, ALU.bypass)
                    vector.tensor_tensor(b["s_go"][:, r * 512:(r + 1) * 512],
                                         b["s_kc"][:, :], b["s_vg"][:, :],
                                         ALU.mult)
                    vector.tensor_scalar(b["s_carry"][:, r:r + 1],
                                         b["s_kc"][:, 511:512], 1.0, None,
                                         ALU.mult).then_inc(v, 1)
                ctr.inc(v, 1)
                tensor.wait_ge(v, ctr.get(v))
            marks[("gdone", ti, grp)] = ctr.get(t_s)
        marks[("tdone", ti)] = ctr.get(t_s)

    for ti in range(6):
        do_ttile(ti, False)
    if subcut <= 14:
        return
    gpsimd = eng["gpsimd"]
    p_s = sems["m_p"]
    gpsimd.wait_ge(v, ctr.get(v))
    gpsimd.wait_ge(sems["m_ldc"], ctr.get(sems["m_ldc"]))
    gpsimd.tensor_tensor(b["s_colsum"][:, 0:48], b["s_colsum"][:, 0:48],
                         b["s_colsum"][:, 48:96], ALU.add)
    gpsimd.tensor_tensor(b["s_colsum"][:, 48:96], b["s_colsum"][:, 0:48],
                         b["s_premask"][:, 0:48], ALU.mult)
    gpsimd.tensor_tensor(b["s_off"][:, :], b["s_colsum"][:, 48:64],
                         b["s_colsum"][:, 64:80], ALU.add)
    gpsimd.tensor_tensor(b["s_off"][:, :], b["s_off"][:, :],
                         b["s_colsum"][:, 80:96], ALU.add).then_inc(p_s, 1)
    ctr.inc(p_s, 1)
    off_p = ctr.get(p_s)
    if cut <= 1:
        return

    vector.wait_ge(sems["m_p"], ctr.get(sems["m_p"]))
    for ti in range(6, 8):
        do_ttile(ti, True)
        tt = ti - 6
        if dbg is not None and ti == 6:
            sync.wait_ge(v, ctr.get(v))
            sync.dma_start(out=dbg["go6"], in_=b["s_go"][:, :]).then_inc(
                sems["m_ldc"], 16)
            sync.dma_start(out=dbg["carry6"], in_=b["s_carry"][:, :]).then_inc(
                sems["m_ldc"], 16)
            ctr.inc(sems["m_ldc"], 32)
        if cut <= 2:
            continue
        sync.wait_ge(sems["m_st0"], marks[("tst", ti, 14)])
        sync.wait_ge(sems["m_st1"], marks[("tst", ti, 15)])
        for dt in range(4):
            wpar = dt % 2
            ldw = sems[f"m_ldw{wpar}"]
            if dt >= 2 or tt > 0:
                sync.wait_ge(t_s, ctr.get(t_s))
            else:
                sync.wait_ge(t_s, marks[("tdone", ti)])
            src = bass.AP(tensor=b["WuT_dram"].tensor, offset=dt * 512,
                          ap=[[2048, 128], [128 * 2048, 16], [1, 512]])
            sync.dma_start(out=b["s_wun"][wpar][:, :], in_=src).then_inc(ldw, 16)
            ctr.inc(ldw, 16)
            wld = ctr.get(ldw)
            for tc in range(4):
                ip = tc % 2
                ldx = sems[f"m_ldx{ip}"]
                if dt + tc > 0:
                    sync.wait_ge(v, ctr.get(v))
                sync.dma_start(out=b["s_toutk"][ip][:, :],
                               in_=toutd[dt * 512:dt * 512 + 512,
                                         tt * 512 + tc * 128:
                                         tt * 512 + tc * 128 + 128],
                               transpose=True).then_inc(ldx, 16)
                sync.dma_start(out=b["s_xsl"][ip][:, :],
                               in_=b["x_own"][1 + tt * 512 + tc * 128:
                                              1 + tt * 512 + tc * 128 + 128,
                                              dt * 512:dt * 512 + 512]).then_inc(ldx, 16)
                ctr.inc(ldx, 32)
                xld = ctr.get(ldx)
                tensor.wait_ge(ldw, wld)
                if dt + tc > 0:
                    tensor.wait_ge(v, ctr.get(v))
                for rch in range(16):
                    mi = tensor.matmul(pb[7][:, :],
                                       b["s_go"][:, rch * 512 + tc * 128:
                                                 rch * 512 + tc * 128 + 128],
                                       b["s_wun"][wpar][:, rch * 512:(rch + 1) * 512],
                                       start=(rch == 0), stop=(rch == 15),
                                       skip_group_check=True)
                mi.then_inc(t_s, 1)
                ctr.inc(t_s, 1)
                vector.wait_ge(t_s, ctr.get(t_s))
                vector.wait_ge(ldx, xld)
                if dt + tc > 0:
                    vector.wait_ge(sems["m_st"], ctr.get(sems["m_st"]))
                vector.tensor_tensor(b["s_ev"][0][:, :], b["s_toutk"][ip][:, :],
                                     pb[7][:, :], ALU.add)
                vector.tensor_tensor(b["s_o"][ip][:, :], b["s_ev"][0][:, :],
                                     b["s_xsl"][ip][:, :], ALU.add).then_inc(v, 1)
                ctr.inc(v, 1)
                marks["uvdone"] = ctr.get(v)
                scalar.wait_ge(v, ctr.get(v))
                scalar.dma_start(
                    out=b["xmid_dram"][tt * 512 + tc * 128:tt * 512 + tc * 128 + 128,
                                       dt * 512:dt * 512 + 512],
                    in_=b["s_o"][ip][:, :]).then_inc(sems["m_st"], 16)
                ctr.inc(sems["m_st"], 16)


def ffn_phase(eng, b, sems, ctr):
    sync, scalar, vector, tensor = (eng["sync"], eng["scalar"], eng["vector"],
                                    eng["tensor"])
    v, a, t_s = sems["m_v"], sems["m_a"], sems["m_t"]
    pb = b["pb"]
    dwt = b["dwT"].rearrange("(i h) -> i h", h=D)
    for snm in ["m_st", "m_st0", "m_st1"]:
        sync.wait_ge(sems[snm], ctr.get(sems[snm]))
    for sup in range(2):
        if sup > 0:
            sync.wait_ge(t_s, ctr.get(t_s))
        ldx = sems["m_ldx0"]
        for k in range(16):
            sync.dma_start(out=b["s_hnf"][:, k * 512:(k + 1) * 512],
                           in_=b["hn_dram"][sup * 512:sup * 512 + 512,
                                            128 * k:128 * k + 128],
                           transpose=True).then_inc(ldx, 16)
            ctr.inc(ldx, 16)
        hn_ld = ctr.get(ldx)
        for ig in range(16):
            wpar = ig % 2
            ldw = sems[f"m_ldw{wpar}"]
            gw, uw = b["s_gw"][wpar], b["s_uw"][wpar]
            if ig >= 2:
                sync.wait_ge(t_s, ctr.get(t_s))
            for pk, dst in [("gwT_pk", gw), ("upT_pk", uw)]:
                srcap = bass.AP(tensor=b[pk].tensor, offset=ig * (16 * 128 * 512),
                                ap=[[512, 128], [128 * 512, 16], [1, 512]])
                sync.dma_start(out=dst[:, :], in_=srcap).then_inc(ldw, 16)
                ctr.inc(ldw, 16)
            wld = ctr.get(ldw)
            tensor.wait_ge(ldw, wld)
            tensor.wait_ge(ldx, hn_ld)
            if ig >= 1:
                tensor.wait_ge(v, ctr.get(v))
            for ic in range(4):
                for k in range(16):
                    tensor.matmul(pb[ic][:, :],
                                  gw[:, k * 512 + ic * 128:k * 512 + ic * 128 + 128],
                                  b["s_hnf"][:, k * 512:(k + 1) * 512],
                                  start=(k == 0), stop=(k == 15),
                                  skip_group_check=True)
                for k in range(16):
                    mi = tensor.matmul(pb[4 + ic][:, :],
                                       uw[:, k * 512 + ic * 128:k * 512 + ic * 128 + 128],
                                       b["s_hnf"][:, k * 512:(k + 1) * 512],
                                       start=(k == 0), stop=(k == 15),
                                       skip_group_check=True)
            mi.then_inc(t_s, 1)
            ctr.inc(t_s, 1)
            scalar.wait_ge(t_s, ctr.get(t_s))
            vector.wait_ge(t_s, ctr.get(t_s))
            vmk = {}
            for ic in range(4):
                i = ig * 4 + ic
                sp = ic % 2
                if ic >= 2:
                    scalar.wait_ge(v, vmk[ic - 2])
                scalar.activation(b["s_silu"][:, sp * 512:sp * 512 + 512],
                                  pb[ic][:, :], AF.Silu).then_inc(a, 1)
                ctr.inc(a, 1)
                vector.wait_ge(a, ctr.get(a))
                vector.tensor_tensor(b["s_inter"][:, i * 512:(i + 1) * 512],
                                     b["s_silu"][:, sp * 512:sp * 512 + 512],
                                     pb[4 + ic][:, :],
                                     ALU.mult).then_inc(v, 1)
                ctr.inc(v, 1)
                vmk[ic] = ctr.get(v)
        inter_v = ctr.get(v)
        for hh in range(2):
            tensor.wait_ge(v, inter_v)
            if hh > 0 or sup > 0:
                tensor.wait_ge(v, ctr.get(v))
            for i in range(64):
                dpar = i % 2
                ldw = sems[f"m_ldw{dpar}"]
                if i >= 2:
                    sync.wait_ge(t_s, marks_dw[i - 2])
                sync.dma_start(out=b["s_dw"][dpar][:, :],
                               in_=dwt[128 * i:128 * i + 128, :]).then_inc(ldw, 16)
                ctr.inc(ldw, 16)
                dld = ctr.get(ldw)
                tensor.wait_ge(ldw, dld)
                for ht in range(2):
                    h4 = hh * 2 + ht
                    for tc in range(4):
                        mi = tensor.matmul(pb[tc * 2 + ht][:, :],
                                           b["s_inter"][:, i * 512 + tc * 128:
                                                        i * 512 + tc * 128 + 128],
                                           b["s_dw"][dpar][:, h4 * 512:h4 * 512 + 512],
                                           start=(i == 0), stop=(i == 63),
                                           skip_group_check=True)
                mi.then_inc(t_s, 1)
                ctr.inc(t_s, 1)
                marks_dw[i] = ctr.get(t_s)
            for ht in range(2):
                h4 = hh * 2 + ht
                for tc in range(4):
                    ip = tc % 2
                    if True:
                        sync.wait_ge(v, ctr.get(v))
                    sync.dma_start(out=b["s_xsl"][ip][:, :],
                                   in_=b["xmid_dram"][sup * 512 + tc * 128:
                                                      sup * 512 + tc * 128 + 128,
                                                      h4 * 512:h4 * 512 + 512])                         .then_inc(ldx, 16)
                    ctr.inc(ldx, 16)
                    vector.wait_ge(ldx, ctr.get(ldx))
                    vector.wait_ge(t_s, ctr.get(t_s))
                    if True:
                        vector.wait_ge(sems["m_st"], ctr.get(sems["m_st"]))
                    vector.tensor_tensor(b["s_o"][ip][:, :],
                                         pb[tc * 2 + ht][:, :],
                                         b["s_xsl"][ip][:, :],
                                         ALU.add).then_inc(v, 1)
                    ctr.inc(v, 1)
                    scalar.wait_ge(v, ctr.get(v))
                    scalar.dma_start(
                        out=b["out"][sup * 512 + tc * 128:sup * 512 + tc * 128 + 128,
                                     h4 * 512:h4 * 512 + 512],
                        in_=b["s_o"][ip][:, :]).then_inc(sems["m_st"], 16)
                    ctr.inc(sems["m_st"], 16)


marks_dw = {}


def finale(eng, sems, ctr):
    for snm in ["m_st", "m_st0", "m_st1"]:
        eng["scalar"].wait_ge(sems[snm], ctr.get(sems[snm]))
        eng["sync"].wait_ge(sems[snm], ctr.get(sems[snm]))


N_CTRL = 130156
NSAMP = INTER * D
RECON_SEMS = ["r_cpx", "r_ldt", "r_ld0", "r_ld1", "r_ab", "r_mm", "r_bl",
              "r_st0", "r_st1"]
Ws = ["g", "u", "d"]


def _replay_all(nc, rec):
    with nc.Block() as block:
        @block.sync
        def _(sync):
            rec.replay("sync", sync)

        @block.vector
        def _(vector):
            rec.replay("vector", vector)

        @block.gpsimd
        def _(gpsimd):
            rec.replay("gpsimd", gpsimd)

        @block.tensor
        def _(tensor):
            rec.replay("tensor", tensor)

        @block.scalar
        def _(scalar):
            rec.replay("scalar", scalar)


def _build_recon(plan, tbl):
    nc = bass.Bass("TRN2", target_bir_lowering=False)
    cp_in = {W: nc.dram_tensor(f"cp_{W}", [N_CTRL], F32,
                               kind="ExternalInput").ap() for W in Ws}
    cpx = {W: nc.dram_tensor(f"cpx_{W}", [N_CTRL + CPX_PAD], F32).ap()
           for W in Ws}
    nrows = plan["nrows"]
    wout = {W: nc.dram_tensor(f"w_{W}", [nrows * ROW], BF16,
                              kind="ExternalOutput").ap() for W in Ws}
    ctx = ExitStack()
    sems = {nm: ctx.enter_context(nc.semaphore(nm)) for nm in RECON_SEMS}
    ctr = Ctr()
    bufs = alloc_recon_bufs(nc, Ws, tbl)
    rec = Recorder()
    eng = rec.eng()
    for W in Ws:
        emit_cpx_build(eng, cp_in[W], cpx[W], N_CTRL, sems, ctr)
    emit_recon_prologue(eng, bufs, sems, ctr)
    emit_recon(eng, bufs, plan, tbl, cpx, wout, sems, ctr)
    _replay_all(nc, rec)
    ctx.close()
    return nc


def _build_main():
    nc = bass.Bass("TRN2", target_bir_lowering=False)
    ctx = ExitStack()
    sems = {nm: ctx.enter_context(nc.semaphore(nm)) for nm in MAIN_SEMS}
    ctr = Ctr()
    b = alloc_main(nc, v0_external_wbf=True)
    rec = Recorder()
    eng = rec.eng()
    prologue(eng, b, sems, ctr)
    ln_stream(eng, b, sems, ctr,
              [(b["x_pre"], PRE, 0), (b["x_own"], OWN + 2, PRE)],
              b["xn_dram"])
    attn_phase(eng, b, sems, ctr)
    ln_stream(eng, b, sems, ctr, [(b["xmid_dram"], OWN, 0)], b["hn_dram"],
              gate_sts=True)
    ffn_phase(eng, b, sems, ctr)
    finale(eng, sems, ctr)
    _replay_all(nc, rec)
    ctx.close()
    return nc


def _pack_core_inputs(c, x, Wu, bu, Wv, bv, Wg1, bg1, Wg2, bg2, cw, cb,
                      gate_bf, up_bf, down_bf):
    bat, blk = c // 4, c % 4
    t0 = blk * 1024
    L = 4096
    x_own = np.zeros((OWN + 2, D), np.float32)
    j0, j1 = max(0, t0 - 1), min(L, t0 + OWN + 1)
    x_own[j0 - (t0 - 1):j1 - (t0 - 1)] = x[bat, j0:j1]
    x_pre = np.zeros((PRE, D), np.float32)
    x_pre[0:t0] = x[bat, 0:t0]
    premask = np.zeros((128, 48), np.float32)
    for w in range(blk):
        premask[:, w * 16:(w + 1) * 16] = 1.0
    biases = np.zeros((128, 80), np.float32)
    biases[:, 0:16] = bu.reshape(16, 128).T
    biases[:, 16:32] = bv.reshape(16, 128).T
    biases[:, 32:48] = bg2.reshape(16, 128).T
    biases[:, 48:64] = cb.reshape(16, 128).T
    biases[:, 64:68] = bg1.reshape(4, 128).T
    convw = np.zeros((128, 48), np.float32)
    for k in range(16):
        convw[:, 3 * k:3 * k + 3] = cw[128 * k:128 * k + 128, :]
    return {
        "x_own": x_own, "x_pre": x_pre, "premask": premask,
        "Wu": Wu, "Wv": Wv, "Wg1": Wg1, "Wg2": Wg2,
        "biases": biases, "conv_w": convw,
        "gate_bf": gate_bf, "up_bf": up_bf, "down_bf": down_bf,
    }


def kernel(**inputs):
    x = np.asarray(inputs["L0_attn_x"], np.float32)
    Wu = np.asarray(inputs["L0_attn_Wu"], np.float32)
    bu = np.asarray(inputs["L0_attn_bu"], np.float32)
    Wv = np.asarray(inputs["L0_attn_Wv"], np.float32)
    bv = np.asarray(inputs["L0_attn_bv"], np.float32)
    Wg1 = np.asarray(inputs["L0_attn_Wg1"], np.float32)
    bg1 = np.asarray(inputs["L0_attn_bg1"], np.float32)
    Wg2 = np.asarray(inputs["L0_attn_Wg2"], np.float32)
    bg2 = np.asarray(inputs["L0_attn_bg2"], np.float32)
    cw = np.asarray(inputs["L0_attn_conv_w"], np.float32)[:, 0, :]
    cb = np.asarray(inputs["L0_attn_conv_b"], np.float32)
    cps = {"g": np.asarray(inputs["L0_ffn_gate_cp"], np.float32),
           "u": np.asarray(inputs["L0_ffn_up_cp"], np.float32),
           "d": np.asarray(inputs["L0_ffn_down_cp"], np.float32)}

    plan = recon_plan(N_CTRL, NSAMP)
    tbl = recon_tables(plan)
    nc_r = _build_recon(plan, tbl)
    im_r = {f"cp_{W}": cps[W] for W in Ws}
    im_r.update(phiq1_t=tbl["PHIQ1"], phiq2_t=tbl["PHIQ2"],
                gstar_t=tbl["gstar"], grow_t=tbl["grow"],
                d1_t=tbl["D1"], d2_t=tbl["D2"])
    res_r = run_bass_kernel_spmd(nc_r, [im_r] * 8, core_ids=list(range(8)))
    pad = np.zeros(256, BF)
    wbf = {W: np.concatenate([np.asarray(res_r.results[0][f"w_{W}"])[:NSAMP],
                              pad]) for W in Ws}

    nc_m = _build_main()
    ims = [_pack_core_inputs(c, x, Wu, bu, Wv, bv, Wg1, bg1, Wg2, bg2,
                             cw, cb, wbf["g"], wbf["u"], wbf["d"])
           for c in range(8)]
    res = run_bass_kernel_spmd(nc_m, ims, core_ids=list(range(8)))
    y = np.zeros((2, 4096, 2048), np.float32)
    for c in range(8):
        bat, blk = c // 4, c % 4
        y[bat, blk * 1024:(blk + 1) * 1024] = np.asarray(res.results[c]["out"])
    return y


def _build_fused(plan, tbl):
    nc = bass.Bass("TRN2", target_bir_lowering=False)
    ctx = ExitStack()
    sems = {nm: ctx.enter_context(nc.semaphore(nm))
            for nm in RECON_SEMS + MAIN_SEMS}
    ctr = Ctr()
    b = alloc_main(nc, v0_external_wbf=False)
    cp_in = {W: nc.dram_tensor(f"cp_{W}", [N_CTRL], F32,
                               kind="ExternalInput").ap() for W in Ws}
    cpx = {W: nc.dram_tensor(f"cpx_{W}", [N_CTRL + CPX_PAD], F32).ap()
           for W in Ws}
    nrows = plan["nrows"]
    wdst = {"g": b["gate_bf"], "u": b["up_bf"], "d": b["down_bf"]}
    wout = {W: wdst[W][0:nrows * ROW] for W in Ws}
    bufs = alloc_recon_bufs(nc, Ws, tbl, pb=b["pb"], arena=b["_Car"])
    rec = Recorder()
    eng = rec.eng()
    for W in Ws:
        emit_cpx_build(eng, cp_in[W], cpx[W], N_CTRL, sems, ctr)
    emit_recon_prologue(eng, bufs, sems, ctr)
    emit_recon(eng, bufs, plan, tbl, cpx, wout, sems, ctr)
    for e in ("sync", "scalar", "vector", "tensor", "gpsimd"):
        eng[e].wait_ge(sems["r_st0"], ctr.get(sems["r_st0"]))
        eng[e].wait_ge(sems["r_st1"], ctr.get(sems["r_st1"]))
    prologue(eng, b, sems, ctr)
    ln_stream(eng, b, sems, ctr,
              [(b["x_pre"], PRE, 0), (b["x_own"], OWN + 2, PRE)],
              b["xn_dram"])
    attn_phase(eng, b, sems, ctr)
    ln_stream(eng, b, sems, ctr, [(b["xmid_dram"], OWN, 0)], b["hn_dram"],
              gate_sts=True)
    ffn_phase(eng, b, sems, ctr)
    finale(eng, sems, ctr)
    _replay_all(nc, rec)
    ctx.close()
    return nc


def kernel(**inputs):
    x = np.asarray(inputs["L0_attn_x"], np.float32)
    Wu = np.asarray(inputs["L0_attn_Wu"], np.float32)
    bu = np.asarray(inputs["L0_attn_bu"], np.float32)
    Wv = np.asarray(inputs["L0_attn_Wv"], np.float32)
    bv = np.asarray(inputs["L0_attn_bv"], np.float32)
    Wg1 = np.asarray(inputs["L0_attn_Wg1"], np.float32)
    bg1 = np.asarray(inputs["L0_attn_bg1"], np.float32)
    Wg2 = np.asarray(inputs["L0_attn_Wg2"], np.float32)
    bg2 = np.asarray(inputs["L0_attn_bg2"], np.float32)
    cw = np.asarray(inputs["L0_attn_conv_w"], np.float32)[:, 0, :]
    cb = np.asarray(inputs["L0_attn_conv_b"], np.float32)
    cps = {"g": np.asarray(inputs["L0_ffn_gate_cp"], np.float32),
           "u": np.asarray(inputs["L0_ffn_up_cp"], np.float32),
           "d": np.asarray(inputs["L0_ffn_down_cp"], np.float32)}

    plan = recon_plan(N_CTRL, NSAMP)
    tbl = recon_tables(plan)
    nc = _build_fused(plan, tbl)
    zero_bf = np.zeros(1, BF)
    ims = []
    for c in range(8):
        im = _pack_core_inputs(c, x, Wu, bu, Wv, bv, Wg1, bg1, Wg2, bg2,
                               cw, cb, zero_bf, zero_bf, zero_bf)
        for k in ("gate_bf", "up_bf", "down_bf"):
            del im[k]
        im.update({f"cp_{W}": cps[W] for W in Ws})
        im.update(phiq1_t=tbl["PHIQ1"], phiq2_t=tbl["PHIQ2"],
                  gstar_t=tbl["gstar"], grow_t=tbl["grow"],
                  d1_t=tbl["D1"], d2_t=tbl["D2"])
        ims.append(im)
    res = run_bass_kernel_spmd(nc, ims, core_ids=list(range(8)))
    y = np.zeros((2, 4096, 2048), np.float32)
    for c in range(8):
        bat, blk = c // 4, c % 4
        y[bat, blk * 1024:(blk + 1) * 1024] = np.asarray(res.results[c]["out"])
    return y


# version 04
# speedup vs baseline: 1.2819x; 1.0063x over previous
import sys
if "/opt/trn_rl_repo" not in sys.path:
    sys.path.insert(0, "/opt/trn_rl_repo")
import numpy as np
import ml_dtypes
import concourse.bass as bass
import concourse.mybir as mybir
from concourse.bass_utils import run_bass_kernel_spmd
from contextlib import ExitStack

F32 = mybir.dt.float32
BF16 = mybir.dt.bfloat16
AF = mybir.ActivationFunctionType
ALU = mybir.AluOpType
BF = ml_dtypes.bfloat16
ROW = 129


def recon_plan(n_ctrl, n):
    t = np.linspace(0.0, n_ctrl - 1.0, n, dtype=np.float64)
    i = np.clip(np.floor(t).astype(np.int64), 0, n_ctrl - 2)
    nrows = (n + ROW - 1) // ROW
    pad = nrows * ROW - n
    if pad:
        i = np.concatenate([i, np.full(pad, i[-1], np.int64)])
        t = np.concatenate([t, np.full(pad, t[-1], np.float64)])
    i_rows = i.reshape(nrows, ROW)
    t_rows = t.reshape(nrows, ROW)
    j_row = i_rows[:, 0]
    phi_row = t_rows[:, 0] - j_row
    delta = i_rows - j_row[:, None]
    assert delta.max() <= 1, "a row crosses more than one interval"
    cross_g = np.full(nrows, ROW, np.int64)
    has = delta[:, -1] == 1
    cross_g[has] = np.argmax(delta[has] > 0, axis=1)
    tiles = []
    r = 0
    while r < nrows:
        j0 = int(j_row[r])
        k = 1
        while r + k < nrows and k < 128 and j_row[r + k] == j0 + k:
            k += 1
        tiles.append([r, k, j0, False])
        r += k
    if len(tiles) % 2:
        tiles.append([nrows, 1, int(j_row[-1]), True])
    return dict(tiles=tiles, phi_row=phi_row, cross_g=cross_g, nrows=nrows,
                n=n, n_ctrl=n_ctrl)


def _c_poly():
    return np.array([
        [0.0, -0.5, 1.0, -0.5],
        [1.0, 0.0, -2.5, 1.5],
        [0.0, 0.5, 2.0, -1.5],
        [0.0, 0.0, -0.5, 0.5],
    ], dtype=np.float64)


def recon_tables(plan):
    n_ctrl, n = plan["n_ctrl"], plan["n"]
    sigma = (n_ctrl - 1.0) / (n - 1.0)
    from math import comb
    C = _c_poly()
    u = np.arange(ROW, dtype=np.float64) * sigma
    D = np.zeros((16, ROW))
    for w in range(4):
        d = np.zeros((4, 4))
        for k in range(4):
            for m in range(k + 1):
                d[m][k - m] += C[w, k] * comb(k, m)
        for m in range(4):
            D[4 * m + w] = sum(d[m][k] * u ** k for k in range(4))
    D = D.astype(np.float32)
    D1 = np.zeros((32, 2 * ROW), np.float32)
    D1[0:16, 0:ROW] = D
    D1[16:32, ROW:2 * ROW] = D
    D2 = np.zeros((64, ROW), np.float32)
    D2[0:16] = D; D2[16:32] = D; D2[32:48] = D; D2[48:64] = D

    tiles = plan["tiles"]
    ntiles = len(tiles)
    npairs = ntiles // 2
    nrows = plan["nrows"]
    phi_pad = np.concatenate([plan["phi_row"], np.zeros(300)])
    cg_pad = np.concatenate([plan["cross_g"], np.full(300, ROW)])
    PHIQ1 = np.zeros((npairs, 32, 128), np.float32)
    PHIQ2 = np.zeros((npairs, 64, 128), np.float32)
    gstar = np.zeros((128, ntiles), np.float32)
    for pi in range(npairs):
        for half in range(2):
            r0, nr, j0, dummy = tiles[2 * pi + half]
            ph = phi_pad[r0:r0 + 128]
            gstar[:, 2 * pi + half] = cg_pad[r0:r0 + 128]
            for m in range(4):
                pm = (ph ** m).astype(np.float32)
                qa = (-(ph ** m)).astype(np.float32)
                qb = ((ph - 1.0) ** m).astype(np.float32)
                for w in range(4):
                    PHIQ1[pi, 16 * half + 4 * m + w] = pm
                    PHIQ2[pi, 32 * half + 4 * m + w] = qa
                    PHIQ2[pi, 32 * half + 16 + 4 * m + w] = qb
    grow = np.broadcast_to(np.arange(ROW, dtype=np.float32), (128, ROW)).copy()
    return dict(D1=D1, D2=D2, PHIQ1=PHIQ1, PHIQ2=PHIQ2, gstar=gstar,
                grow=grow, sigma=sigma, npairs=npairs)


def make_cpx(cp):
    raise NotImplementedError


class Ctr:
    def __init__(self):
        self.v = {}

    def key(self, sem):
        return getattr(sem, "name", None) or id(sem)

    def inc(self, sem, n=1):
        self.v[self.key(sem)] = self.get(sem) + n

    def get(self, sem):
        return self.v.get(self.key(sem), 0)


def alloc_recon_bufs(nc, Ws, tbl, pb=None, arena=None):
    b = {}
    if arena is None:
        def mk(name, shape, dt):
            return mk(name, shape, dt)
    else:
        arF = arena.bitcast(F32)
        arB = arena
        state = {"byte": 0}

        def mk(name, shape, dt):
            P, C = shape
            b0 = (state["byte"] + 3) & ~3
            w = 4 if dt == F32 else 2
            state["byte"] = b0 + C * w
            if dt == F32:
                return arF[0:P, b0 // 4:b0 // 4 + C]
            return arB[0:P, b0 // 2:b0 // 2 + C]
    b["phiq1_dram"] = nc.dram_tensor("phiq1_t", list(tbl["PHIQ1"].shape), F32,
                                     kind="ExternalInput").ap()
    b["phiq2_dram"] = nc.dram_tensor("phiq2_t", list(tbl["PHIQ2"].shape), F32,
                                     kind="ExternalInput").ap()
    b["gstar_dram"] = nc.dram_tensor("gstar_t", list(tbl["gstar"].shape), F32,
                                     kind="ExternalInput").ap()
    b["grow_dram"] = nc.dram_tensor("grow_t", [128, ROW], F32,
                                    kind="ExternalInput").ap()
    b["d1_dram"] = nc.dram_tensor("d1_t", [32, 2 * ROW], F32,
                                  kind="ExternalInput").ap()
    b["d2_dram"] = nc.dram_tensor("d2_t", [64, ROW], F32,
                                  kind="ExternalInput").ap()
    b["s_phiq1"] = [mk(f"s_phiq1_{p}", [32, 128], F32)
                    for p in range(2)]
    b["s_phiq2"] = [mk(f"s_phiq2_{p}", [64, 128], F32)
                    for p in range(2)]
    b["s_cpw1"] = [{W: mk(f"s_cpw1_{p}_{W}", [32, 128], F32)
                    for W in Ws} for p in range(2)]
    b["s_cpw2"] = [{W: mk(f"s_cpw2_{p}_{W}", [64, 128], F32)
                    for W in Ws} for p in range(2)]
    b["s_a1"] = [{W: mk(f"s_a1_{p}_{W}", [32, 128], BF16)
                  for W in Ws} for p in range(2)]
    b["s_a2"] = [{W: mk(f"s_a2_{p}_{W}", [64, 128], BF16)
                  for W in Ws} for p in range(2)]
    b["s_d1"] = mk("s_d1", [32, 2 * ROW], F32)
    b["s_d2"] = mk("s_d2", [64, ROW], F32)
    b["s_d1b"] = mk("s_d1b", [32, 2 * ROW], BF16)
    b["s_d2b"] = mk("s_d2b", [64, ROW], BF16)
    b["s_grow"] = mk("s_grow", [128, ROW], F32)
    ntl = tbl["gstar"].shape[1]
    b["s_gstar"] = mk("s_gstar", [128, ntl], F32)
    b["s_mask"] = [mk(f"s_mask{p}", [128, 2 * ROW], F32)
                   for p in range(2)]
    b["s_t"] = mk("s_t", [128, 2 * ROW], F32)
    b["s_out"] = [{W: mk(f"s_out_{p}_{W}", [128, 2 * ROW], BF16)
                   for W in Ws} for p in range(2)]
    if pb is None:
        pb = [nc.alloc_psum_tensor(f"pbank{k}", [128, 512], F32)
              for k in range(8)]
    b["pbanks"] = pb
    b["p1"] = [{W: pb[p * 3 + wi][:, 0:258] for wi, W in enumerate(Ws)}
               for p in range(2)]
    b["p2a"] = [{W: pb[p * 3 + wi][:, 258:387] for wi, W in enumerate(Ws)}
                for p in range(2)]
    b["p2b"] = [{W: pb[6 + p][:, wi * 129:(wi + 1) * 129]
                 for wi, W in enumerate(Ws)} for p in range(2)]
    return b


def emit_recon_prologue(eng, bufs, sems, ctr):
    sync, vector = eng["sync"], eng["vector"]
    ldt, ab = sems["r_ldt"], sems["r_ab"]
    sync.dma_start(out=bufs["s_grow"][:, :], in_=bufs["grow_dram"]).then_inc(ldt, 16)
    sync.dma_start(out=bufs["s_gstar"][:, :], in_=bufs["gstar_dram"]).then_inc(ldt, 16)
    sync.dma_start(out=bufs["s_d1"][:, :], in_=bufs["d1_dram"]).then_inc(ldt, 16)
    sync.dma_start(out=bufs["s_d2"][:, :], in_=bufs["d2_dram"]).then_inc(ldt, 16)
    ctr.inc(ldt, 64)
    vector.wait_ge(ldt, ctr.get(ldt))
    vector.tensor_copy(bufs["s_d1b"][:, :], bufs["s_d1"][:, :])
    vector.tensor_copy(bufs["s_d2b"][:, :], bufs["s_d2"][:, :]).then_inc(ab, 1)
    ctr.inc(ab, 1)


def emit_recon(eng, bufs, plan, tbl, cp_aps, out_aps, sems, ctr,
               max_pairs=None):
    sync, vector, gpsimd, tensor = (eng["sync"], eng["vector"],
                                    eng["gpsimd"], eng["tensor"])
    tiles = plan["tiles"]
    npairs = tbl["npairs"] if max_pairs is None else min(tbl["npairs"], max_pairs)
    Ws = list(cp_aps.keys())
    NW = len(Ws)
    ab, mm, bl = sems["r_ab"], sems["r_mm"], sems["r_bl"]
    lds = [sems["r_ld0"], sems["r_ld1"]]
    sts = [sems["r_st0"], sems["r_st1"]]
    st_tot = []

    for pi in range(npairs):
        par = pi % 2
        r0a, nra, j0a, duma = tiles[2 * pi]
        r0b, nrb, j0b, dumb = tiles[2 * pi + 1]
        dj = j0b - j0a

        ld = lds[par]
        st = sts[par]
        scalarq = eng["scalar"]
        qs = [sync, scalarq, gpsimd]
        for q in (sync, scalarq):
            if pi == 0:
                q.wait_ge(sems["r_cpx"], ctr.get(sems["r_cpx"]))
            if pi >= 2:
                q.wait_ge(ab, 1 + (pi - 1))
        sync.dma_start(out=bufs["s_phiq1"][par][:, :],
                       in_=bufs["phiq1_dram"][pi, :, :]).then_inc(ld, 16)
        sync.dma_start(out=bufs["s_phiq2"][par][:, :],
                       in_=bufs["phiq2_dram"][pi, :, :]).then_inc(ld, 16)
        ctr.inc(ld, 32)
        for wi, W in enumerate(Ws):
            q = qs[wi]
            cpx = cp_aps[W]
            for half, j0 in enumerate([j0a, j0b]):
                win = bass.AP(tensor=cpx.tensor, offset=cpx.offset + j0,
                              ap=[[0, 4], [1, 4], [1, 128]])
                q.dma_start(
                    out=bufs["s_cpw1"][par][W][16 * half:16 * half + 16, :],
                    in_=win).then_inc(ld, 16)
                winB = bass.AP(tensor=cpx.tensor, offset=cpx.offset + j0 + 1,
                               ap=[[0, 4], [1, 4], [1, 128]])
                q.dma_start(
                    out=bufs["s_cpw2"][par][W][32 * half:32 * half + 16, :],
                    in_=win).then_inc(ld, 16)
                q.dma_start(
                    out=bufs["s_cpw2"][par][W][32 * half + 16:32 * half + 32, :],
                    in_=winB).then_inc(ld, 16)
                ctr.inc(ld, 48)

        gpsimd.wait_ge(ld, ctr.get(ld))
        if pi >= 2:
            gpsimd.wait_ge(mm, pi - 1)
        for wi, W in enumerate(Ws):
            gpsimd.tensor_tensor(bufs["s_a1"][par][W][:, :],
                                 bufs["s_cpw1"][par][W][:, :],
                                 bufs["s_phiq1"][par][:, :], ALU.mult)
            ins = gpsimd.tensor_tensor(bufs["s_a2"][par][W][:, :],
                                       bufs["s_cpw2"][par][W][:, :],
                                       bufs["s_phiq2"][par][:, :], ALU.mult)
            if wi == NW - 1:
                ins.then_inc(ab, 1)
        ctr.inc(ab, 1)

        tensor.wait_ge(ab, ctr.get(ab))
        if pi >= 2:
            tensor.wait_ge(bl, pi - 1)
        for wi, W in enumerate(Ws):
            tensor.matmul(bufs["p1"][par][W], bufs["s_a1"][par][W][0:32, :],
                          bufs["s_d1b"][:, :], start=True, stop=True,
                          skip_group_check=True)
            tensor.matmul(bufs["p2a"][par][W], bufs["s_a2"][par][W][0:32, :],
                          bufs["s_d2b"][0:32, :], start=True, stop=True,
                          skip_group_check=True)
            ins = tensor.matmul(bufs["p2b"][par][W], bufs["s_a2"][par][W][32:64, :],
                                bufs["s_d2b"][32:64, :], start=True, stop=True,
                                skip_group_check=True)
            if wi == NW - 1:
                ins.then_inc(mm, 1)
        ctr.inc(mm, 1)

        vector.wait_ge(mm, ctr.get(mm))
        if pi >= 2:
            vector.wait_ge(st, st_tot[pi - 2])
        mk = bufs["s_mask"][par]
        vector.tensor_scalar(mk[:, 0:ROW], bufs["s_grow"][:, :],
                             bufs["s_gstar"][:, 2 * pi:2 * pi + 1], None, ALU.is_ge)
        vector.tensor_scalar(mk[:, ROW:2 * ROW], bufs["s_grow"][:, :],
                             bufs["s_gstar"][:, 2 * pi + 1:2 * pi + 2], None,
                             ALU.is_ge)
        for wi, W in enumerate(Ws):
            t_s = bufs["s_t"]
            vector.tensor_tensor(t_s[:, 0:ROW], mk[:, 0:ROW],
                                 bufs["p2a"][par][W], ALU.mult)
            vector.tensor_tensor(t_s[:, ROW:2 * ROW], mk[:, ROW:2 * ROW],
                                 bufs["p2b"][par][W], ALU.mult)
            ins = vector.tensor_add(bufs["s_out"][par][W][:, :], t_s[:, :],
                                    bufs["p1"][par][W])
            if wi == NW - 1:
                ins.then_inc(bl, 1)
        ctr.inc(bl, 1)

        sync.wait_ge(bl, ctr.get(bl))
        for W in Ws:
            out = out_aps[W]
            for half, (r0, nr, dum) in enumerate(
                    [(r0a, nra, duma), (r0b, nrb, dumb)]):
                if dum:
                    continue
                src = bufs["s_out"][par][W][0:nr, half * ROW:(half + 1) * ROW]
                dst = bass.AP(tensor=out.tensor, offset=out.offset + r0 * ROW,
                              ap=[[ROW, nr], [1, ROW]])
                sync.dma_start(out=dst, in_=src).then_inc(st, 16)
                ctr.inc(st, 16)
        st_tot.append(ctr.get(st))

    sync.wait_ge(st, ctr.get(st))


CPX_PAD = 140


def emit_cpx_build(eng, cp_ap, cpx_ap, n_ctrl, sems, ctr):
    sync = eng["sync"]
    ld = sems["r_cpx"]
    sync.dma_start(out=cpx_ap[1:n_ctrl + 1], in_=cp_ap[:]).then_inc(ld, 16)
    sync.dma_start(out=cpx_ap[0:1], in_=cp_ap[0:1]).then_inc(ld, 16)
    npad = CPX_PAD - 1
    last = bass.AP(tensor=cp_ap.tensor, offset=cp_ap.offset + n_ctrl - 1,
                   ap=[[0, npad], [1, 1]])
    sync.dma_start(out=cpx_ap[n_ctrl + 1:n_ctrl + CPX_PAD]
                   .rearrange('(a b) -> a b', b=1), in_=last).then_inc(ld, 16)
    ctr.inc(ld, 48)


class _RecCall:
    def __init__(self, entry):
        self.entry = entry

    def then_inc(self, sem, n=1):
        self.entry[3].append((sem, n))
        return self


class _RecEngine:
    def __init__(self, script):
        self.script = script

    def __getattr__(self, meth):
        def call(*args, **kw):
            entry = [meth, args, kw, []]
            self.script.append(entry)
            return _RecCall(entry)
        return call


class Recorder:
    ENGINES = ("sync", "vector", "gpsimd", "tensor", "scalar")

    def __init__(self):
        self.scripts = {e: [] for e in self.ENGINES}

    def eng(self):
        return {e: _RecEngine(self.scripts[e]) for e in self.ENGINES}

    def replay(self, name, engine):
        for meth, args, kw, incs in self.scripts[name]:
            ins = getattr(engine, meth)(*args, **kw)
            for sem, n in incs:
                ins = ins.then_inc(sem, n)



D = 2048
R = 2048
DH = 512
INTER = 8192
OWN = 1024
PRE = 3072
NTOK = PRE + 2 + OWN
EPS = 1e-5

MAIN_SEMS = ["m_ldx0", "m_ldx1", "m_ldw0", "m_ldw1", "m_ldc", "m_st",
             "m_st0", "m_st1", "m_v", "m_a", "m_p", "m_t"]


def alloc_main(nc, v0_external_wbf=True):
    b = {}
    d = b
    d["x_own"] = nc.dram_tensor("x_own", [OWN + 2, D], F32, kind="ExternalInput").ap()
    d["x_pre"] = nc.dram_tensor("x_pre", [PRE, D], F32, kind="ExternalInput").ap()
    d["premask"] = nc.dram_tensor("premask", [128, 48], F32, kind="ExternalInput").ap()
    d["Wu"] = nc.dram_tensor("Wu", [R, D], F32, kind="ExternalInput").ap()
    d["Wv"] = nc.dram_tensor("Wv", [R, D], F32, kind="ExternalInput").ap()
    d["Wg1"] = nc.dram_tensor("Wg1", [DH, D], F32, kind="ExternalInput").ap()
    d["Wg2"] = nc.dram_tensor("Wg2", [R, DH], F32, kind="ExternalInput").ap()
    d["biases"] = nc.dram_tensor("biases", [128, 5 * 16], F32, kind="ExternalInput").ap()
    d["conv_w"] = nc.dram_tensor("conv_w", [128, 16 * 3], F32, kind="ExternalInput").ap()
    kind = "ExternalInput" if v0_external_wbf else "Internal"
    d["gate_bf"] = nc.dram_tensor("gate_bf", [INTER * D + 256], BF16, kind=kind).ap()
    d["up_bf"] = nc.dram_tensor("up_bf", [INTER * D + 256], BF16, kind=kind).ap()
    d["down_bf"] = nc.dram_tensor("down_bf", [INTER * D + 256], BF16, kind=kind).ap()
    d["out"] = nc.dram_tensor("out", [OWN, D], F32, kind="ExternalOutput").ap()

    d["Wu_bf"] = nc.dram_tensor("Wu_bf", [R, D], BF16).ap()
    d["Wv_bf"] = nc.dram_tensor("Wv_bf", [R, D], BF16).ap()
    d["Wg1_bf"] = nc.dram_tensor("Wg1_bf", [DH, D], BF16).ap()
    d["Wg2_bf"] = nc.dram_tensor("Wg2_bf", [R, DH], BF16).ap()
    d["WuT_pk"] = nc.dram_tensor("WuT_pk", [4, 16, 128, 512], BF16).ap()
    d["WvT_pk"] = nc.dram_tensor("WvT_pk", [4, 16, 128, 512], BF16).ap()
    d["WuT_dram"] = nc.dram_tensor("WuT_dram", [R, D], BF16).ap()
    d["xn_dram"] = nc.dram_tensor("xn_dram", [NTOK + 32, D], BF16).ap()
    d["tout_dram"] = nc.dram_tensor("tout_dram", [D * OWN], BF16).ap()
    d["xmid_dram"] = nc.dram_tensor("xmid_dram", [OWN, D], F32).ap()
    d["hn_dram"] = nc.dram_tensor("hn_dram", [OWN, D], BF16).ap()
    d["gwT_pk"] = nc.dram_tensor("gwT_pk", [16, 16, 128, 512], BF16).ap()
    d["upT_pk"] = nc.dram_tensor("upT_pk", [16, 16, 128, 512], BF16).ap()
    d["dwT"] = nc.dram_tensor("dwT", [INTER * D], BF16).ap()

    sb = nc.alloc_sbuf_tensor
    A = sb("arenaA", [128, 16384], F32)
    A16 = A.ap().bitcast(BF16)
    A32 = A.ap()
    Bar = sb("arenaB", [128, 8192], BF16).ap()
    Car = sb("arenaC", [128, 16896], BF16).ap()
    Ear = sb("arenaE", [128, 16384], BF16).ap()
    d["s_x"] = [A32[:, 0:2048], A32[:, 2048:4096]]
    d["s_sqv"] = A32[:, 4096:6144]
    d["s_sqa"] = A32[:, 6144:8192]
    d["s_xn"] = [A16[:, 16384:18432], A16[:, 18432:20480]]
    d["s_wupk"] = [A16[:, 0:8192], A16[:, 8192:16384]]
    d["s_wv"] = [A16[:, 16384:24576], A16[:, 24576:32768]]
    d["s_wun"] = [A16[:, 0:8192], A16[:, 8192:16384]]
    d["s_inter"] = A16[:, 0:32768]
    d["s_wtr"] = Bar[:, 0:8192]
    d["s_go"] = Bar[:, 0:8192]
    d["s_xnf"] = [Car[:, 0:8448], Car[:, 8448:16896]]
    d["_Car"] = Car
    d["s_gw"] = [Car[:, 0:8192], Car[:, 8192:16384]]
    d["s_wg1t"] = Ear[:, 0:8192]
    d["s_wg2t"] = Ear[:, 8192:16384]
    d["s_uw"] = [Ear[:, 0:8192], Ear[:, 8192:16384]]
    d["s_hnf"] = sb("s_hnf", [128, 16 * 512], BF16)
    d["s_lncol"] = sb("s_lncol", [128, 32], F32)
    d["s_premask"] = sb("s_premask", [128, 48], F32)
    d["s_colsum"] = sb("s_colsum", [128, 96], F32)
    d["s_off"] = sb("s_off", [128, 16], F32)
    d["s_carry"] = sb("s_carry", [128, 16], F32)
    d["s_bias"] = sb("s_bias", [128, 5 * 16], F32)
    d["s_convw"] = sb("s_convw", [128, 16 * 3], F32)
    d["s_usb"] = sb("s_usb", [128, 512], F32)
    d["s_gsb"] = sb("s_gsb", [128, 512], F32)
    d["s_vsb"] = sb("s_vsb", [128, 512], F32)
    d["s_ug"] = sb("s_ug", [128, 512], F32)
    d["s_vg"] = sb("s_vg", [128, 512], BF16)
    d["s_kc"] = sb("s_kc", [128, 512], F32)
    d["s_hf"] = sb("s_hf", [128, 4 * 512], BF16)
    d["s_hraw"] = sb("s_hraw", [128, 512], F32)
    d["s_ev"] = [sb(f"s_ev{i}", [128, 512], F32) for i in range(2)]
    d["s_tout"] = [sb(f"s_tout{i}", [128, 512], BF16) for i in range(2)]
    d["s_toutk"] = [sb(f"s_toutk{i}", [128, 512], BF16) for i in range(2)]
    d["s_xsl"] = [sb(f"s_xsl{i}", [128, 512], F32) for i in range(2)]
    d["s_dw"] = [sb(f"s_dw{i}", [128, D], BF16) for i in range(2)]
    d["s_silu"] = sb("s_silu", [128, 1024], F32)
    d["s_o"] = [sb(f"s_o{i}", [128, 512], F32) for i in range(2)]
    d["pb"] = [nc.alloc_psum_tensor(f"mpb{k}", [128, 512], F32) for k in range(8)]
    return b


def prologue(eng, b, sems, ctr, wbf_ready=0):
    sync, scalar, gpsimd = eng["sync"], eng["scalar"], eng["gpsimd"]
    ldc, ldw, st, g = sems["m_ldc"], sems["m_ldw0"], sems["m_st"], sems["m_p"]
    sync.dma_start(out=b["s_bias"][:, :], in_=b["biases"]).then_inc(ldc, 16)
    sync.dma_start(out=b["s_convw"][:, :], in_=b["conv_w"]).then_inc(ldc, 16)
    sync.dma_start(out=b["s_premask"][:, :], in_=b["premask"]).then_inc(ldc, 16)
    ctr.inc(ldc, 48)
    for w, wb in [("Wu", "Wu_bf"), ("Wv", "Wv_bf"), ("Wg1", "Wg1_bf"),
                  ("Wg2", "Wg2_bf")]:
        gpsimd.dma_start(out=b[wb], in_=b[w]).then_inc(g, 16)
        ctr.inc(g, 16)
    pdone = ctr.get(g)
    sync.wait_ge(g, pdone)

    def tr_store(in_ap, stage_cols, dst_ap):
        sync.wait_ge(st, ctr.get(st))
        sync.dma_start(out=b["s_wtr"][:, 0:stage_cols], in_=in_ap,
                       transpose=True).then_inc(ldw, 16)
        ctr.inc(ldw, 16)
        scalar.wait_ge(ldw, ctr.get(ldw))
        src = b["s_wtr"][:, 0:stage_cols]
        scalar.dma_start(out=dst_ap, in_=src).then_inc(st, 16)
        ctr.inc(st, 16)

    for k in range(16):
        sync.dma_start(out=b["s_wg1t"][:, k * DH:(k + 1) * DH],
                       in_=b["Wg1_bf"][0:DH, 128 * k:128 * k + 128],
                       transpose=True).then_inc(ldw, 16)
        ctr.inc(ldw, 16)
    for k in range(4):
        sync.dma_start(out=b["s_wg2t"][:, k * R:(k + 1) * R],
                       in_=b["Wg2_bf"][0:R, 128 * k:128 * k + 128],
                       transpose=True).then_inc(ldw, 16)
        ctr.inc(ldw, 16)
    for w, pk in [("Wu_bf", "WuT_pk"), ("Wv_bf", "WvT_pk")]:
        for k in range(16):
            dst = bass.AP(tensor=b[pk].tensor, offset=k * (128 * 512),
                          ap=[[512, 128], [16 * 128 * 512, 4], [1, 512]])
            sync.wait_ge(st, ctr.get(st))
            sync.dma_start(out=b["s_wtr"][:, 0:R],
                           in_=b[w][0:R, 128 * k:128 * k + 128],
                           transpose=True).then_inc(ldw, 16)
            ctr.inc(ldw, 16)
            scalar.wait_ge(ldw, ctr.get(ldw))
            src = b["s_wtr"][:, 0:R].rearrange("p (g c) -> p g c", g=4)
            scalar.dma_start(out=dst, in_=src).then_inc(st, 16)
            ctr.inc(st, 16)
    for k in range(16):
        tr_store(b["Wu_bf"][0:D, 128 * k:128 * k + 128], D,
                 b["WuT_dram"][128 * k:128 * k + 128, :])
    if True:
        gw = b["gate_bf"][0:INTER * D].rearrange("(i h) -> i h", h=D)
        uw = b["up_bf"][0:INTER * D].rearrange("(i h) -> i h", h=D)
        dw = b["down_bf"][0:INTER * D].rearrange("(h i) -> h i", i=INTER)
        dwt = b["dwT"].rearrange("(i h) -> i h", h=D)
        for wsrc, pk in [(gw, "gwT_pk"), (uw, "upT_pk")]:
            for k in range(16):
                dst = bass.AP(tensor=b[pk].tensor, offset=k * (128 * 512),
                              ap=[[512, 128], [16 * 128 * 512, 16], [1, 512]])
                sync.wait_ge(st, ctr.get(st))
                sync.dma_start(out=b["s_wtr"][:, :],
                               in_=wsrc[0:INTER, 128 * k:128 * k + 128],
                               transpose=True).then_inc(ldw, 16)
                ctr.inc(ldw, 16)
                scalar.wait_ge(ldw, ctr.get(ldw))
                src = b["s_wtr"][:, :].rearrange("p (g c) -> p g c", g=16)
                scalar.dma_start(out=dst, in_=src).then_inc(st, 16)
                ctr.inc(st, 16)
        for k in range(64):
            tr_store(dw[0:D, 128 * k:128 * k + 128], D,
                     dwt[128 * k:128 * k + 128, :])


def ln_stream(eng, b, sems, ctr, srcs, xn_dst, gate_sts=False):
    sync, scalar, vector = eng["sync"], eng["scalar"], eng["vector"]
    pool = eng["gpsimd"]
    v, a, p = sems["m_v"], sems["m_a"], sems["m_p"]
    if gate_sts:
        for snm in ["m_st", "m_st0", "m_st1"]:
            sync.wait_ge(sems[snm], ctr.get(sems[snm]))
    scalar.wait_ge(sems["m_ldc"], ctr.get(sems["m_ldc"]))
    tiles = []
    for ap, n, drow in srcs:
        r = 0
        while r < n:
            nr = min(128, n - r)
            tiles.append((ap, r, nr, drow + r))
            r += nr
    vmark, stmark = {}, {}
    for t, (ap, r0, nr, orow) in enumerate(tiles):
        par = t % 2
        ldx = sems[f"m_ldx{par}"]
        st = sems[f"m_st{par}"]
        sx, sxn, col = b["s_x"][par], b["s_xn"][par], b["s_lncol"]
        cb = 16 * par
        c = lambda k: col[0:nr, cb + k:cb + k + 1]
        if t >= 2:
            sync.wait_ge(v, vmark[t - 2])
        sync.dma_start(out=sx[0:nr, :], in_=ap[r0:r0 + nr, :]).then_inc(ldx, 16)
        ctr.inc(ldx, 16)
        ldn = ctr.get(ldx)
        scalar.wait_ge(ldx, ldn)
        scalar.activation(b["s_sqv"][0:nr, :], sx[0:nr, :], AF.Identity,
                          accum_out=c(0))
        scalar.activation(b["s_sqa"][0:nr, :], sx[0:nr, :], AF.Square,
                          accum_out=c(1)).then_inc(a, 1)
        ctr.inc(a, 1)
        vector.wait_ge(a, ctr.get(a))
        vector.tensor_scalar(c(2), c(0), -1.0 / D, None, ALU.mult)
        vector.tensor_scalar(c(5), c(1), 1.0 / D, EPS, ALU.mult,
                             ALU.add).then_inc(v, 1)
        ctr.inc(v, 1)
        scalar.wait_ge(v, ctr.get(v))
        scalar.activation(c(3), c(2), AF.Square).then_inc(a, 1)
        ctr.inc(a, 1)
        pool.wait_ge(a, ctr.get(a))
        pool.tensor_tensor(c(4), c(5), c(3), ALU.subtract).then_inc(p, 1)
        ctr.inc(p, 1)
        scalar.wait_ge(p, ctr.get(p))
        scalar.activation(c(6), c(4), AF.Ln).then_inc(a, 1)
        ctr.inc(a, 1)
        vector.wait_ge(a, ctr.get(a))
        vector.tensor_scalar(c(7), c(6), -0.5, None, ALU.mult).then_inc(v, 1)
        ctr.inc(v, 1)
        scalar.wait_ge(v, ctr.get(v))
        scalar.activation(c(8), c(7), AF.Exp).then_inc(a, 1)
        ctr.inc(a, 1)
        pool.wait_ge(a, ctr.get(a))
        pool.tensor_tensor(c(9), c(2), c(8), ALU.mult).then_inc(p, 1)
        ctr.inc(p, 1)
        vector.wait_ge(p, ctr.get(p))
        if t >= 2:
            vector.wait_ge(st, stmark[t - 2])
        vector.tensor_scalar(sxn[0:nr, :], sx[0:nr, :], c(8),
                             c(9), ALU.mult, ALU.add).then_inc(v, 1)
        ctr.inc(v, 1)
        vmark[t] = ctr.get(v)
        scalar.wait_ge(v, ctr.get(v))
        scalar.dma_start(out=xn_dst[orow:orow + nr, :],
                         in_=sxn[0:nr, :]).then_inc(st, 16)
        ctr.inc(st, 16)
        stmark[t] = ctr.get(st)


def attn_phase(eng, b, sems, ctr, cut=3, subcut=99, dbg=None):
    sync, scalar, vector, tensor = (eng["sync"], eng["scalar"], eng["vector"],
                                    eng["tensor"])
    v, a, t_s = sems["m_v"], sems["m_a"], sems["m_t"]
    pb = b["pb"]
    for snm in ["m_st", "m_st0", "m_st1"]:
        sync.wait_ge(sems[snm], ctr.get(sems[snm]))
    scalar.wait_ge(sems["m_ldc"], ctr.get(sems["m_ldc"]))
    vector.wait_ge(sems["m_ldc"], ctr.get(sems["m_ldc"]))
    toutd = b["tout_dram"].rearrange("(d t) -> d t", t=OWN)
    marks = {"uvdone": 0}
    if cut <= 0:
        return

    def do_ttile(ti, own):
        par = ti % 2
        ldx = sems[f"m_ldx{par}"]
        xnf = b["s_xnf"][par]
        if own:
            row0 = PRE + (ti - 6) * 512
            width = 528
        else:
            row0 = ti * 512
            width = 512
        if ti >= 2:
            sync.wait_ge(t_s, marks[("tdone", ti - 2)])
        for k in range(16):
            sync.dma_start(out=xnf[:, k * 528:k * 528 + width],
                           in_=b["xn_dram"][row0:row0 + width,
                                            128 * k:128 * k + 128],
                           transpose=True).then_inc(ldx, 16)
            ctr.inc(ldx, 16)
        xn_ld = ctr.get(ldx)
        off = 1 if own else 0
        if subcut <= 11:
            marks[("tdone", ti)] = ctr.get(t_s)
            return

        if own:
            for k in range(16):
                tp = k % 2
                stp = sems[f"m_st{tp}"]
                vector.wait_ge(ldx, xn_ld)
                if k >= 2:
                    vector.wait_ge(stp, marks[("tst", ti, k - 2)])
                cw = b["s_convw"]
                vector.tensor_scalar(b["s_ev"][0][:, :],
                                     xnf[:, k * 528:k * 528 + 512],
                                     cw[:, 3 * k:3 * k + 1],
                                     b["s_bias"][:, 48 + k:48 + k + 1],
                                     ALU.mult, ALU.add)
                vector.scalar_tensor_tensor(b["s_ev"][1][:, :],
                                            xnf[:, k * 528 + 1:k * 528 + 513],
                                            cw[:, 3 * k + 1:3 * k + 2],
                                            b["s_ev"][0][:, :], ALU.mult, ALU.add)
                vector.scalar_tensor_tensor(b["s_tout"][tp][:, :],
                                            xnf[:, k * 528 + 2:k * 528 + 514],
                                            cw[:, 3 * k + 2:3 * k + 3],
                                            b["s_ev"][1][:, :], ALU.mult,
                                            ALU.add).then_inc(v, 1)
                ctr.inc(v, 1)
                scalar.wait_ge(v, ctr.get(v))
                scalar.dma_start(
                    out=toutd[128 * k:128 * k + 128,
                              (ti - 6) * 512:(ti - 6) * 512 + 512],
                    in_=b["s_tout"][tp][:, :]).then_inc(stp, 16)
                ctr.inc(stp, 16)
                marks[("tst", ti, k)] = ctr.get(stp)

        tensor.wait_ge(ldx, xn_ld)
        tensor.wait_ge(v, ctr.get(v))
        if ti >= 7:
            tensor.wait_ge(v, marks["uvdone"])
        for m in range(4):
            for k in range(16):
                mi = tensor.matmul(pb[4 + m][:, :],
                                   b["s_wg1t"][:, k * DH + m * 128:k * DH + m * 128 + 128],
                                   xnf[:, k * 528 + off:k * 528 + off + 512],
                                   start=(k == 0), stop=(k == 15),
                                   skip_group_check=True)
        mi.then_inc(t_s, 1)
        ctr.inc(t_s, 1)
        if subcut <= 12:
            marks[("tdone", ti)] = ctr.get(t_s)
            return
        scalar.wait_ge(t_s, ctr.get(t_s))
        for m in range(4):
            scalar.activation(b["s_hraw"][:, :], pb[4 + m][:, :], AF.Identity,
                              bias=b["s_bias"][:, 64 + m:64 + m + 1])
            ins = scalar.activation(b["s_hf"][:, m * 512:(m + 1) * 512],
                                    b["s_hraw"][:, :], AF.Relu)
        ins.then_inc(a, 1)
        ctr.inc(a, 1)
        a_h = ctr.get(a)
        if subcut <= 13:
            tensor.wait_ge(a, a_h)
            marks[("tdone", ti)] = ctr.get(t_s)
            return

        for grp in range(4):
            wpar = grp % 2
            ldw = sems[f"m_ldw{wpar}"]
            wu = b["s_wupk"][wpar]
            wv = b["s_wv"][wpar]
            if grp >= 2:
                sync.wait_ge(t_s, marks[("gdone", ti, grp - 2)])
            elif ti == 7:
                sync.wait_ge(v, marks["uvdone"])
            elif ti >= 1:
                sync.wait_ge(t_s, marks[("gdone", ti - 1, grp + 2)])
            src = bass.AP(tensor=b["WuT_pk"].tensor, offset=grp * (16 * 128 * 512),
                          ap=[[512, 128], [128 * 512, 16], [1, 512]])
            sync.dma_start(out=wu[:, :], in_=src).then_inc(ldw, 16)
            ctr.inc(ldw, 16)
            if own:
                srcv = bass.AP(tensor=b["WvT_pk"].tensor,
                               offset=grp * (16 * 128 * 512),
                               ap=[[512, 128], [128 * 512, 16], [1, 512]])
                sync.dma_start(out=wv[:, :], in_=srcv).then_inc(ldw, 16)
                ctr.inc(ldw, 16)
            w_ld = ctr.get(ldw)
            tensor.wait_ge(ldw, w_ld)
            for j in range(4):
                r = grp * 4 + j
                for k in range(16):
                    tensor.matmul(pb[j][:, :],
                                  wu[:, k * 512 + j * 128:k * 512 + j * 128 + 128],
                                  xnf[:, k * 528 + off:k * 528 + off + 512],
                                  start=(k == 0), stop=(k == 15),
                                  skip_group_check=True)
                if own:
                    tensor.wait_ge(a, a_h)
                    if not (grp == 0 and j == 0):
                        tensor.wait_ge(v, ctr.get(v))
                    for k in range(16):
                        tensor.matmul(pb[4][:, :],
                                      wv[:, k * 512 + j * 128:k * 512 + j * 128 + 128],
                                      xnf[:, k * 528 + off:k * 528 + off + 512],
                                      start=(k == 0), stop=(k == 15),
                                      skip_group_check=True)
                tensor.wait_ge(a, a_h)
                for m in range(4):
                    mi = tensor.matmul(pb[5][:, :],
                                       b["s_wg2t"][:, m * R + r * 128:m * R + r * 128 + 128],
                                       b["s_hf"][:, m * 512:(m + 1) * 512],
                                       start=(m == 0), stop=(m == 3),
                                       skip_group_check=True)
                mi.then_inc(t_s, 1)
                ctr.inc(t_s, 1)
                scalar.wait_ge(t_s, ctr.get(t_s))
                scalar.activation(b["s_usb"][:, :], pb[j][:, :], AF.Identity,
                                  bias=b["s_bias"][:, r:r + 1])
                if own:
                    scalar.activation(b["s_vsb"][:, :], pb[4][:, :],
                                      AF.Identity,
                                      bias=b["s_bias"][:, 16 + r:16 + r + 1])
                scalar.activation(b["s_gsb"][:, :], pb[5][:, :], AF.Sigmoid,
                                  bias=b["s_bias"][:, 32 + r:32 + r + 1]
                                  ).then_inc(a, 1)
                ctr.inc(a, 1)
                if subcut <= 14:
                    tensor.wait_ge(a, ctr.get(a))
                    continue
                vector.wait_ge(a, ctr.get(a))
                if not own:
                    blk = ti // 2
                    half = ti % 2
                    cs = b["s_colsum"][:, half * 48 + blk * 16 + r:
                                       half * 48 + blk * 16 + r + 1]
                    vector.scalar_tensor_tensor(
                        b["s_ug"][:, :], b["s_usb"][:, :], 1.0, b["s_gsb"][:, :],
                        ALU.mult, ALU.mult, accum_out=cs).then_inc(v, 1)
                else:
                    vector.scalar_tensor_tensor(
                        b["s_ug"][:, :], b["s_usb"][:, :], 1.0, b["s_gsb"][:, :],
                        ALU.mult, ALU.mult)
                    vector.tensor_tensor(b["s_vg"][:, :], b["s_vsb"][:, :],
                                         b["s_gsb"][:, :], ALU.mult)
                    init = (b["s_off"][:, r:r + 1] if ti == 6
                            else b["s_carry"][:, r:r + 1])
                    vector.tensor_tensor_scan(b["s_kc"][:, :], b["s_ug"][:, :],
                                              b["s_ug"][:, :], init,
                                              ALU.add, ALU.bypass)
                    vector.tensor_tensor(b["s_go"][:, r * 512:(r + 1) * 512],
                                         b["s_kc"][:, :], b["s_vg"][:, :],
                                         ALU.mult)
                    vector.tensor_scalar(b["s_carry"][:, r:r + 1],
                                         b["s_kc"][:, 511:512], 1.0, None,
                                         ALU.mult).then_inc(v, 1)
                ctr.inc(v, 1)
                tensor.wait_ge(v, ctr.get(v))
            marks[("gdone", ti, grp)] = ctr.get(t_s)
        marks[("tdone", ti)] = ctr.get(t_s)

    for ti in range(6):
        do_ttile(ti, False)
    if subcut <= 14:
        return
    gpsimd = eng["gpsimd"]
    p_s = sems["m_p"]
    gpsimd.wait_ge(v, ctr.get(v))
    gpsimd.wait_ge(sems["m_ldc"], ctr.get(sems["m_ldc"]))
    gpsimd.tensor_tensor(b["s_colsum"][:, 0:48], b["s_colsum"][:, 0:48],
                         b["s_colsum"][:, 48:96], ALU.add)
    gpsimd.tensor_tensor(b["s_colsum"][:, 48:96], b["s_colsum"][:, 0:48],
                         b["s_premask"][:, 0:48], ALU.mult)
    gpsimd.tensor_tensor(b["s_off"][:, :], b["s_colsum"][:, 48:64],
                         b["s_colsum"][:, 64:80], ALU.add)
    gpsimd.tensor_tensor(b["s_off"][:, :], b["s_off"][:, :],
                         b["s_colsum"][:, 80:96], ALU.add).then_inc(p_s, 1)
    ctr.inc(p_s, 1)
    off_p = ctr.get(p_s)
    if cut <= 1:
        return

    vector.wait_ge(sems["m_p"], ctr.get(sems["m_p"]))
    for ti in range(6, 8):
        do_ttile(ti, True)
        tt = ti - 6
        if dbg is not None and ti == 6:
            sync.wait_ge(v, ctr.get(v))
            sync.dma_start(out=dbg["go6"], in_=b["s_go"][:, :]).then_inc(
                sems["m_ldc"], 16)
            sync.dma_start(out=dbg["carry6"], in_=b["s_carry"][:, :]).then_inc(
                sems["m_ldc"], 16)
            ctr.inc(sems["m_ldc"], 32)
        if cut <= 2:
            continue
        sync.wait_ge(sems["m_st0"], marks[("tst", ti, 14)])
        sync.wait_ge(sems["m_st1"], marks[("tst", ti, 15)])
        for dt in range(4):
            wpar = dt % 2
            ldw = sems[f"m_ldw{wpar}"]
            if dt >= 2 or tt > 0:
                sync.wait_ge(t_s, ctr.get(t_s))
            else:
                sync.wait_ge(t_s, marks[("tdone", ti)])
            src = bass.AP(tensor=b["WuT_dram"].tensor, offset=dt * 512,
                          ap=[[2048, 128], [128 * 2048, 16], [1, 512]])
            sync.dma_start(out=b["s_wun"][wpar][:, :], in_=src).then_inc(ldw, 16)
            ctr.inc(ldw, 16)
            wld = ctr.get(ldw)
            for tc in range(4):
                ip = tc % 2
                ldx = sems[f"m_ldx{ip}"]
                if dt + tc > 0:
                    sync.wait_ge(v, ctr.get(v))
                sync.dma_start(out=b["s_toutk"][ip][:, :],
                               in_=toutd[dt * 512:dt * 512 + 512,
                                         tt * 512 + tc * 128:
                                         tt * 512 + tc * 128 + 128],
                               transpose=True).then_inc(ldx, 16)
                sync.dma_start(out=b["s_xsl"][ip][:, :],
                               in_=b["x_own"][1 + tt * 512 + tc * 128:
                                              1 + tt * 512 + tc * 128 + 128,
                                              dt * 512:dt * 512 + 512]).then_inc(ldx, 16)
                ctr.inc(ldx, 32)
                xld = ctr.get(ldx)
                tensor.wait_ge(ldw, wld)
                if dt + tc > 0:
                    tensor.wait_ge(v, ctr.get(v))
                for rch in range(16):
                    mi = tensor.matmul(pb[7][:, :],
                                       b["s_go"][:, rch * 512 + tc * 128:
                                                 rch * 512 + tc * 128 + 128],
                                       b["s_wun"][wpar][:, rch * 512:(rch + 1) * 512],
                                       start=(rch == 0), stop=(rch == 15),
                                       skip_group_check=True)
                mi.then_inc(t_s, 1)
                ctr.inc(t_s, 1)
                vector.wait_ge(t_s, ctr.get(t_s))
                vector.wait_ge(ldx, xld)
                if dt + tc > 0:
                    vector.wait_ge(sems["m_st"], ctr.get(sems["m_st"]))
                vector.tensor_tensor(b["s_ev"][0][:, :], b["s_toutk"][ip][:, :],
                                     pb[7][:, :], ALU.add)
                vector.tensor_tensor(b["s_o"][ip][:, :], b["s_ev"][0][:, :],
                                     b["s_xsl"][ip][:, :], ALU.add).then_inc(v, 1)
                ctr.inc(v, 1)
                marks["uvdone"] = ctr.get(v)
                scalar.wait_ge(v, ctr.get(v))
                scalar.dma_start(
                    out=b["xmid_dram"][tt * 512 + tc * 128:tt * 512 + tc * 128 + 128,
                                       dt * 512:dt * 512 + 512],
                    in_=b["s_o"][ip][:, :]).then_inc(sems["m_st"], 16)
                ctr.inc(sems["m_st"], 16)


def ffn_phase(eng, b, sems, ctr):
    sync, scalar, vector, tensor = (eng["sync"], eng["scalar"], eng["vector"],
                                    eng["tensor"])
    v, a, t_s = sems["m_v"], sems["m_a"], sems["m_t"]
    pb = b["pb"]
    dwt = b["dwT"].rearrange("(i h) -> i h", h=D)
    for snm in ["m_st", "m_st0", "m_st1"]:
        sync.wait_ge(sems[snm], ctr.get(sems[snm]))
    for sup in range(2):
        if sup > 0:
            sync.wait_ge(t_s, ctr.get(t_s))
        ldx = sems["m_ldx0"]
        for k in range(16):
            sync.dma_start(out=b["s_hnf"][:, k * 512:(k + 1) * 512],
                           in_=b["hn_dram"][sup * 512:sup * 512 + 512,
                                            128 * k:128 * k + 128],
                           transpose=True).then_inc(ldx, 16)
            ctr.inc(ldx, 16)
        hn_ld = ctr.get(ldx)
        for ig in range(16):
            wpar = ig % 2
            ldw = sems[f"m_ldw{wpar}"]
            gw, uw = b["s_gw"][wpar], b["s_uw"][wpar]
            if ig >= 2:
                sync.wait_ge(t_s, ctr.get(t_s))
            for pk, dst in [("gwT_pk", gw), ("upT_pk", uw)]:
                srcap = bass.AP(tensor=b[pk].tensor, offset=ig * (16 * 128 * 512),
                                ap=[[512, 128], [128 * 512, 16], [1, 512]])
                sync.dma_start(out=dst[:, :], in_=srcap).then_inc(ldw, 16)
                ctr.inc(ldw, 16)
            wld = ctr.get(ldw)
            tensor.wait_ge(ldw, wld)
            tensor.wait_ge(ldx, hn_ld)
            if ig >= 1:
                tensor.wait_ge(v, ctr.get(v))
            for ic in range(4):
                for k in range(16):
                    tensor.matmul(pb[ic][:, :],
                                  gw[:, k * 512 + ic * 128:k * 512 + ic * 128 + 128],
                                  b["s_hnf"][:, k * 512:(k + 1) * 512],
                                  start=(k == 0), stop=(k == 15),
                                  skip_group_check=True)
                for k in range(16):
                    mi = tensor.matmul(pb[4 + ic][:, :],
                                       uw[:, k * 512 + ic * 128:k * 512 + ic * 128 + 128],
                                       b["s_hnf"][:, k * 512:(k + 1) * 512],
                                       start=(k == 0), stop=(k == 15),
                                       skip_group_check=True)
            mi.then_inc(t_s, 1)
            ctr.inc(t_s, 1)
            scalar.wait_ge(t_s, ctr.get(t_s))
            vector.wait_ge(t_s, ctr.get(t_s))
            vmk = {}
            for ic in range(4):
                i = ig * 4 + ic
                sp = ic % 2
                if ic >= 2:
                    scalar.wait_ge(v, vmk[ic - 2])
                scalar.activation(b["s_silu"][:, sp * 512:sp * 512 + 512],
                                  pb[ic][:, :], AF.Silu).then_inc(a, 1)
                ctr.inc(a, 1)
                vector.wait_ge(a, ctr.get(a))
                vector.tensor_tensor(b["s_inter"][:, i * 512:(i + 1) * 512],
                                     b["s_silu"][:, sp * 512:sp * 512 + 512],
                                     pb[4 + ic][:, :],
                                     ALU.mult).then_inc(v, 1)
                ctr.inc(v, 1)
                vmk[ic] = ctr.get(v)
        inter_v = ctr.get(v)
        for hh in range(2):
            tensor.wait_ge(v, inter_v)
            if hh > 0 or sup > 0:
                tensor.wait_ge(v, ctr.get(v))
            for i in range(64):
                dpar = i % 2
                ldw = sems[f"m_ldw{dpar}"]
                if i >= 2:
                    sync.wait_ge(t_s, marks_dw[i - 2])
                sync.dma_start(out=b["s_dw"][dpar][:, :],
                               in_=dwt[128 * i:128 * i + 128, :]).then_inc(ldw, 16)
                ctr.inc(ldw, 16)
                dld = ctr.get(ldw)
                tensor.wait_ge(ldw, dld)
                for ht in range(2):
                    h4 = hh * 2 + ht
                    for tc in range(4):
                        mi = tensor.matmul(pb[tc * 2 + ht][:, :],
                                           b["s_inter"][:, i * 512 + tc * 128:
                                                        i * 512 + tc * 128 + 128],
                                           b["s_dw"][dpar][:, h4 * 512:h4 * 512 + 512],
                                           start=(i == 0), stop=(i == 63),
                                           skip_group_check=True)
                mi.then_inc(t_s, 1)
                ctr.inc(t_s, 1)
                marks_dw[i] = ctr.get(t_s)
            for ht in range(2):
                h4 = hh * 2 + ht
                for tc in range(4):
                    ip = tc % 2
                    if True:
                        sync.wait_ge(v, ctr.get(v))
                    sync.dma_start(out=b["s_xsl"][ip][:, :],
                                   in_=b["xmid_dram"][sup * 512 + tc * 128:
                                                      sup * 512 + tc * 128 + 128,
                                                      h4 * 512:h4 * 512 + 512])                         .then_inc(ldx, 16)
                    ctr.inc(ldx, 16)
                    vector.wait_ge(ldx, ctr.get(ldx))
                    vector.wait_ge(t_s, ctr.get(t_s))
                    if True:
                        vector.wait_ge(sems["m_st"], ctr.get(sems["m_st"]))
                    vector.tensor_tensor(b["s_o"][ip][:, :],
                                         pb[tc * 2 + ht][:, :],
                                         b["s_xsl"][ip][:, :],
                                         ALU.add).then_inc(v, 1)
                    ctr.inc(v, 1)
                    scalar.wait_ge(v, ctr.get(v))
                    scalar.dma_start(
                        out=b["out"][sup * 512 + tc * 128:sup * 512 + tc * 128 + 128,
                                     h4 * 512:h4 * 512 + 512],
                        in_=b["s_o"][ip][:, :]).then_inc(sems["m_st"], 16)
                    ctr.inc(sems["m_st"], 16)


marks_dw = {}


def finale(eng, sems, ctr):
    for snm in ["m_st", "m_st0", "m_st1"]:
        eng["scalar"].wait_ge(sems[snm], ctr.get(sems[snm]))
        eng["sync"].wait_ge(sems[snm], ctr.get(sems[snm]))


N_CTRL = 130156
NSAMP = INTER * D
RECON_SEMS = ["r_cpx", "r_ldt", "r_ld0", "r_ld1", "r_ab", "r_mm", "r_bl",
              "r_st0", "r_st1"]
Ws = ["g", "u", "d"]


def _replay_all(nc, rec):
    with nc.Block() as block:
        @block.sync
        def _(sync):
            rec.replay("sync", sync)

        @block.vector
        def _(vector):
            rec.replay("vector", vector)

        @block.gpsimd
        def _(gpsimd):
            rec.replay("gpsimd", gpsimd)

        @block.tensor
        def _(tensor):
            rec.replay("tensor", tensor)

        @block.scalar
        def _(scalar):
            rec.replay("scalar", scalar)


def _build_recon(plan, tbl):
    nc = bass.Bass("TRN2", target_bir_lowering=False)
    cp_in = {W: nc.dram_tensor(f"cp_{W}", [N_CTRL], F32,
                               kind="ExternalInput").ap() for W in Ws}
    cpx = {W: nc.dram_tensor(f"cpx_{W}", [N_CTRL + CPX_PAD], F32).ap()
           for W in Ws}
    nrows = plan["nrows"]
    wout = {W: nc.dram_tensor(f"w_{W}", [nrows * ROW], BF16,
                              kind="ExternalOutput").ap() for W in Ws}
    ctx = ExitStack()
    sems = {nm: ctx.enter_context(nc.semaphore(nm)) for nm in RECON_SEMS}
    ctr = Ctr()
    bufs = alloc_recon_bufs(nc, Ws, tbl)
    rec = Recorder()
    eng = rec.eng()
    for W in Ws:
        emit_cpx_build(eng, cp_in[W], cpx[W], N_CTRL, sems, ctr)
    emit_recon_prologue(eng, bufs, sems, ctr)
    emit_recon(eng, bufs, plan, tbl, cpx, wout, sems, ctr)
    _replay_all(nc, rec)
    ctx.close()
    return nc


def _build_main():
    nc = bass.Bass("TRN2", target_bir_lowering=False)
    ctx = ExitStack()
    sems = {nm: ctx.enter_context(nc.semaphore(nm)) for nm in MAIN_SEMS}
    ctr = Ctr()
    b = alloc_main(nc, v0_external_wbf=True)
    rec = Recorder()
    eng = rec.eng()
    prologue(eng, b, sems, ctr)
    ln_stream(eng, b, sems, ctr,
              [(b["x_pre"], PRE, 0), (b["x_own"], OWN + 2, PRE)],
              b["xn_dram"])
    attn_phase(eng, b, sems, ctr)
    ln_stream(eng, b, sems, ctr, [(b["xmid_dram"], OWN, 0)], b["hn_dram"],
              gate_sts=True)
    ffn_phase(eng, b, sems, ctr)
    finale(eng, sems, ctr)
    _replay_all(nc, rec)
    ctx.close()
    return nc


def _pack_core_inputs(c, x, Wu, bu, Wv, bv, Wg1, bg1, Wg2, bg2, cw, cb,
                      gate_bf, up_bf, down_bf):
    bat, blk = c // 4, c % 4
    t0 = blk * 1024
    L = 4096
    x_own = np.zeros((OWN + 2, D), np.float32)
    j0, j1 = max(0, t0 - 1), min(L, t0 + OWN + 1)
    x_own[j0 - (t0 - 1):j1 - (t0 - 1)] = x[bat, j0:j1]
    x_pre = np.zeros((PRE, D), np.float32)
    x_pre[0:t0] = x[bat, 0:t0]
    premask = np.zeros((128, 48), np.float32)
    for w in range(blk):
        premask[:, w * 16:(w + 1) * 16] = 1.0
    biases = np.zeros((128, 80), np.float32)
    biases[:, 0:16] = bu.reshape(16, 128).T
    biases[:, 16:32] = bv.reshape(16, 128).T
    biases[:, 32:48] = bg2.reshape(16, 128).T
    biases[:, 48:64] = cb.reshape(16, 128).T
    biases[:, 64:68] = bg1.reshape(4, 128).T
    convw = np.zeros((128, 48), np.float32)
    for k in range(16):
        convw[:, 3 * k:3 * k + 3] = cw[128 * k:128 * k + 128, :]
    return {
        "x_own": x_own, "x_pre": x_pre, "premask": premask,
        "Wu": Wu, "Wv": Wv, "Wg1": Wg1, "Wg2": Wg2,
        "biases": biases, "conv_w": convw,
        "gate_bf": gate_bf, "up_bf": up_bf, "down_bf": down_bf,
    }


def kernel(**inputs):
    x = np.asarray(inputs["L0_attn_x"], np.float32)
    Wu = np.asarray(inputs["L0_attn_Wu"], np.float32)
    bu = np.asarray(inputs["L0_attn_bu"], np.float32)
    Wv = np.asarray(inputs["L0_attn_Wv"], np.float32)
    bv = np.asarray(inputs["L0_attn_bv"], np.float32)
    Wg1 = np.asarray(inputs["L0_attn_Wg1"], np.float32)
    bg1 = np.asarray(inputs["L0_attn_bg1"], np.float32)
    Wg2 = np.asarray(inputs["L0_attn_Wg2"], np.float32)
    bg2 = np.asarray(inputs["L0_attn_bg2"], np.float32)
    cw = np.asarray(inputs["L0_attn_conv_w"], np.float32)[:, 0, :]
    cb = np.asarray(inputs["L0_attn_conv_b"], np.float32)
    cps = {"g": np.asarray(inputs["L0_ffn_gate_cp"], np.float32),
           "u": np.asarray(inputs["L0_ffn_up_cp"], np.float32),
           "d": np.asarray(inputs["L0_ffn_down_cp"], np.float32)}

    plan = recon_plan(N_CTRL, NSAMP)
    tbl = recon_tables(plan)
    nc_r = _build_recon(plan, tbl)
    im_r = {f"cp_{W}": cps[W] for W in Ws}
    im_r.update(phiq1_t=tbl["PHIQ1"], phiq2_t=tbl["PHIQ2"],
                gstar_t=tbl["gstar"], grow_t=tbl["grow"],
                d1_t=tbl["D1"], d2_t=tbl["D2"])
    res_r = run_bass_kernel_spmd(nc_r, [im_r] * 8, core_ids=list(range(8)))
    pad = np.zeros(256, BF)
    wbf = {W: np.concatenate([np.asarray(res_r.results[0][f"w_{W}"])[:NSAMP],
                              pad]) for W in Ws}

    nc_m = _build_main()
    ims = [_pack_core_inputs(c, x, Wu, bu, Wv, bv, Wg1, bg1, Wg2, bg2,
                             cw, cb, wbf["g"], wbf["u"], wbf["d"])
           for c in range(8)]
    res = run_bass_kernel_spmd(nc_m, ims, core_ids=list(range(8)))
    y = np.zeros((2, 4096, 2048), np.float32)
    for c in range(8):
        bat, blk = c // 4, c % 4
        y[bat, blk * 1024:(blk + 1) * 1024] = np.asarray(res.results[c]["out"])
    return y


def _build_fused(plan, tbl):
    nc = bass.Bass("TRN2", target_bir_lowering=False)
    ctx = ExitStack()
    sems = {nm: ctx.enter_context(nc.semaphore(nm))
            for nm in RECON_SEMS + MAIN_SEMS}
    ctr = Ctr()
    b = alloc_main(nc, v0_external_wbf=False)
    cp_in = {W: nc.dram_tensor(f"cp_{W}", [N_CTRL], F32,
                               kind="ExternalInput").ap() for W in Ws}
    cpx = {W: nc.dram_tensor(f"cpx_{W}", [N_CTRL + CPX_PAD], F32).ap()
           for W in Ws}
    nrows = plan["nrows"]
    wdst = {"g": b["gate_bf"], "u": b["up_bf"], "d": b["down_bf"]}
    wout = {W: wdst[W][0:nrows * ROW] for W in Ws}
    bufs = alloc_recon_bufs(nc, Ws, tbl, pb=b["pb"], arena=b["_Car"])
    rec = Recorder()
    eng = rec.eng()
    for W in Ws:
        emit_cpx_build(eng, cp_in[W], cpx[W], N_CTRL, sems, ctr)
    emit_recon_prologue(eng, bufs, sems, ctr)
    emit_recon(eng, bufs, plan, tbl, cpx, wout, sems, ctr)
    for e in ("sync", "scalar", "vector", "tensor", "gpsimd"):
        eng[e].wait_ge(sems["r_st0"], ctr.get(sems["r_st0"]))
        eng[e].wait_ge(sems["r_st1"], ctr.get(sems["r_st1"]))
    prologue(eng, b, sems, ctr)
    ln_stream(eng, b, sems, ctr,
              [(b["x_pre"], PRE, 0), (b["x_own"], OWN + 2, PRE)],
              b["xn_dram"])
    attn_phase(eng, b, sems, ctr)
    ln_stream(eng, b, sems, ctr, [(b["xmid_dram"], OWN, 0)], b["hn_dram"],
              gate_sts=True)
    ffn_phase(eng, b, sems, ctr)
    finale(eng, sems, ctr)
    _replay_all(nc, rec)
    ctx.close()
    return nc


def kernel(**inputs):
    x = np.asarray(inputs["L0_attn_x"], np.float32)
    Wu = np.asarray(inputs["L0_attn_Wu"], np.float32)
    bu = np.asarray(inputs["L0_attn_bu"], np.float32)
    Wv = np.asarray(inputs["L0_attn_Wv"], np.float32)
    bv = np.asarray(inputs["L0_attn_bv"], np.float32)
    Wg1 = np.asarray(inputs["L0_attn_Wg1"], np.float32)
    bg1 = np.asarray(inputs["L0_attn_bg1"], np.float32)
    Wg2 = np.asarray(inputs["L0_attn_Wg2"], np.float32)
    bg2 = np.asarray(inputs["L0_attn_bg2"], np.float32)
    cw = np.asarray(inputs["L0_attn_conv_w"], np.float32)[:, 0, :]
    cb = np.asarray(inputs["L0_attn_conv_b"], np.float32)
    cps = {"g": np.asarray(inputs["L0_ffn_gate_cp"], np.float32),
           "u": np.asarray(inputs["L0_ffn_up_cp"], np.float32),
           "d": np.asarray(inputs["L0_ffn_down_cp"], np.float32)}

    plan = recon_plan(N_CTRL, NSAMP)
    tbl = recon_tables(plan)
    nc = _build_fused(plan, tbl)
    zero_bf = np.zeros(1, BF)
    ims = []
    for c in range(8):
        im = _pack_core_inputs(c, x, Wu, bu, Wv, bv, Wg1, bg1, Wg2, bg2,
                               cw, cb, zero_bf, zero_bf, zero_bf)
        for k in ("gate_bf", "up_bf", "down_bf"):
            del im[k]
        im.update({f"cp_{W}": cps[W] for W in Ws})
        im.update(phiq1_t=tbl["PHIQ1"], phiq2_t=tbl["PHIQ2"],
                  gstar_t=tbl["gstar"], grow_t=tbl["grow"],
                  d1_t=tbl["D1"], d2_t=tbl["D2"])
        ims.append(im)
    res = run_bass_kernel_spmd(nc, ims, core_ids=list(range(8)))
    y = np.zeros((2, 4096, 2048), np.float32)
    for c in range(8):
        bat, blk = c // 4, c % 4
        y[bat, blk * 1024:(blk + 1) * 1024] = np.asarray(res.results[c]["out"])
    return y
